# version 05
# baseline (speedup 1.0000x reference)
import numpy as np
import ml_dtypes
from contextlib import ExitStack
import concourse.bass as bass
import concourse.mybir as mybir
from concourse.bass_utils import run_bass_kernel_spmd

F32 = mybir.dt.float32
BF16 = mybir.dt.bfloat16
AF = mybir.ActivationFunctionType
ALU = mybir.AluOpType
AX = mybir.AxisListType
BF = ml_dtypes.bfloat16

T = 4096
NTT = 32
D = 1024
CH = 256
NCH = 16
DIN = 2048
ENGS = ("sync", "gpsimd", "tensor", "scalar", "vector")
RG = [[0, 1, 2, 3], [4, 5, 6, 7]]


class Sem:
    def __init__(self, nc, es, name):
        self.h = es.enter_context(nc.semaphore(name))
        self.v = 0


class Ctx:
    def __init__(self, nc):
        self.nc = nc
        self.es = ExitStack()
        self.scopes = [self.es]
        self._sems = {}
        self._ps = {}
        self.t = {}
        self.uid = 0

    def sem(self, name):
        if name not in self._sems:
            self._sems[name] = Sem(self.nc, self.es, name)
        return self._sems[name]

    def push_scope(self):
        s = ExitStack()
        self.scopes.append(s)
        return s

    def pop_scope(self):
        self.scopes.pop().close()

    def sb(self, name, shape, dtype, scope=None):
        self.uid += 1
        h = (scope or self.scopes[-1]).enter_context(
            self.nc.sbuf_tensor(f"{name}_{self.uid}", shape, dtype))
        self.t[name] = h
        return h

    def ps(self, name, shape, dtype, scope=None):
        if name not in self._ps:
            self._ps[name] = self.es.enter_context(self.nc.psum_tensor(name, shape, dtype))
        return self._ps[name]


class Ph:

    def __init__(self, ctx):
        self.ctx = ctx
        self.q = {e: [] for e in ENGS}

    def wait(self, eng, sem, v=None):
        vv = sem.v if v is None else v
        self.q[eng].append(lambda e, h=sem.h, vv=vv: e.wait_ge(h, vv))

    def op(self, eng, fn, inc=None, amt=1):
        if inc is not None:
            inc.v += amt
            self.q[eng].append(
                lambda e, fn=fn, h=inc.h, amt=amt: fn(e).then_inc(h, amt))
        else:
            self.q[eng].append(lambda e, fn=fn: fn(e))

    def dma(self, eng, out, in_, sem, **kw):
        sem.v += 16
        self.q[eng].append(
            lambda e, out=out, in_=in_, h=sem.h, kw=kw:
                e.dma_start(out=out, in_=in_, **kw).then_inc(h, 16))
        return sem.v

    def close(self):
        with self.ctx.nc.Block() as blk:
            for e in ENGS:
                if self.q[e]:
                    def runner(eng, q=tuple(self.q[e])):
                        for fn in q:
                            fn(eng)
                    getattr(blk, e)(runner)


def ln_emit(ph, sres, outap, cols, scr, pe, ve, ac, epsap=None, snap=None):
    ph.op("vector", lambda e: e.reduce_sum(cols[:, 0:1], sres, axis=AX.X), inc=ve)
    ph.wait("scalar", ve)
    ph.op("scalar", lambda e: e.activation(scr, sres, AF.Square,
                                           accum_out=cols[:, 1:2]), inc=ac)
    ph.wait("vector", ac)
    ph.op("vector", lambda e: e.tensor_scalar_mul(cols[:, 2:3], cols[:, 0:1], 1.0 / 1024))
    ph.op("vector", lambda e: e.tensor_scalar_mul(cols[:, 3:4], cols[:, 1:2], 1.0 / 1024), inc=ve)
    ph.wait("vector", ve)
    ph.op("vector", lambda e: e.tensor_mul(cols[:, 4:5], cols[:, 2:3], cols[:, 2:3]), inc=ve)
    ph.wait("vector", ve)
    ph.op("vector", lambda e: e.tensor_sub(cols[:, 5:6], cols[:, 3:4], cols[:, 4:5]), inc=ve)
    if snap is not None:
        ph.op("vector", lambda e: e.tensor_copy(snap, cols[:, 0:8]), inc=ve)
    ph.wait("scalar", ve)
    ph.op("scalar", lambda e: e.activation(cols[:, 6:7], cols[:, 5:6], AF.Sqrt,
                                           bias=epsap), inc=ac)
    ph.wait("vector", ac)
    ph.op("vector", lambda e: e.reciprocal(cols[:, 7:8], cols[:, 6:7]), inc=ve)
    ph.wait("vector", ve)
    ph.op("vector", lambda e: e.tensor_scalar(
        outap, sres, cols[:, 2:3], cols[:, 7:8], ALU.subtract, ALU.mult), inc=ve)


def _env(env):
    if env is not None:
        return env
    nc = bass.Bass("TRN2", target_bir_lowering=False,
                   detect_race_conditions=False)
    ctx = Ctx(nc)

    def dram(n, s, d, k):
        return nc.dram_tensor(n, s, d,
                              kind="ExternalInput" if k == "in" else "ExternalOutput").ap()
    return nc, ctx, dram



def build_l0attn(dbg=False, env=None):
    nc, ctx, dram = _env(env)
    ei = lambda n, s, d: dram(n, s, d, "in")
    eo = lambda n, s, d: dram(n, s, d, "out")

    xinT = ei("xinT", [D, 4 + T], F32)
    xin = ei("xin", [T, D], F32)
    w_in = ei("w_in", [D, 4384], F32)
    cdiag_h = ei("cdiag", [72, 128, 128], F32)
    convb = ei("convb", [128, 18], F32)
    dtb = ei("dtb", [32, 1], F32)
    alog = ei("alog", [32, 1], F32)
    drep = ei("drep", [128, DIN], F32)
    nrmrep = ei("nrmrep", [128, DIN], F32)
    w_out = ei("w_out", [DIN, D], F32)
    identb = ei("identb", [128, 128], BF16)
    identf = ei("identf", [128, 128], F32)
    maskT = ei("maskT", [128, 256], BF16)
    cmask = ei("cmask", [32, 3], F32)
    m2 = ei("m2", [32, 12], F32)
    onesf = ei("onesf", [1, 128], F32)

    x1 = eo("x1", [T, D], BF16)

    yhbm = nc.dram_tensor("yhbm", [T, DIN], BF16,
                          kind="ExternalOutput" if dbg else "Internal").ap()
    agsrc = nc.dram_tensor("agsrc", [128, 2052], F32).ap()
    agdst = nc.dram_tensor("agdst", [512, 2052], F32).ap()

    cvb = ctx.sb("cvb", [128, 18], F32)
    dtbc = ctx.sb("dtbc", [32, 1], F32)
    acolb = ctx.sb("acolb", [32, 1], F32)
    dbc = ctx.sb("dbc", [128, DIN], BF16)
    idb = ctx.sb("idb", [128, 128], BF16)
    idf = ctx.sb("idf", [128, 128], F32)
    mskb = ctx.sb("mskb", [128, 256], BF16)
    onf = ctx.sb("onf", [1, 128], F32)
    cmb = ctx.sb("cmb", [32, 3], F32)
    m2b = ctx.sb("m2b", [32, 12], F32)
    ybuf = ctx.sb("ybuf", [128, 4 * DIN], BF16)
    csv = ctx.sb("csv", [128, T], BF16)
    datm = ctx.sb("datm", [128, NTT * 96], F32)
    atot = ctx.sb("atot", [32, NCH], F32)
    runacs = ctx.sb("runacs", [32, 1], F32)
    rst = ctx.sb("rst", [128, DIN], F32)
    rstbf = ctx.sb("rstbf", [128, DIN], BF16)
    scry = ctx.sb("scry", [128, 1024], F32)
    zz32 = ctx.sb("zz32", [32, CH], F32)
    xfmb = ctx.sb("xfmb", [128, 2 * 8 * 260], BF16)
    grow = ctx.sb("grow", [1, 32], F32)
    epsc = ctx.sb("epsc", [128, 1], F32)

    PF = ctx.ps("PF", [128, 2048], F32)
    PG = ctx.ps("PG", [128, 1024], F32)
    PTb = ctx.ps("PTb", [128, 2048], BF16)

    dw = ctx.sem("dw")
    dx = ctx.sem("dx")
    ds = ctx.sem("ds")
    pe = ctx.sem("pe")
    ve = ctx.sem("ve")
    ac = ctx.sem("ac")
    cc = ctx.sem("cc")
    st = ctx.sem("st")
    po = ctx.sem("po")

    xfm = lambda par, dc: xfmb[:, par * 2080 + dc * 260: par * 2080 + (dc + 1) * 260]

    chs = ExitStack()
    wbc = ctx.sb("wbc", [128, 8 * 2336], BF16, chs)
    wdt = ctx.sb("wdt", [128, 8 * 32], BF16, chs)
    cdg = ctx.sb("cdg", [128, 72 * 128], BF16, chs)
    xbcc = ctx.sb("xbcc", [128, 18 * 260], BF16, chs)
    bfm = ctx.sb("bfm", [128, 256], BF16, chs)
    btm = ctx.sb("btm", [128, 256], BF16, chs)
    xstm = ctx.sb("xstm", [128, 2 * DIN], BF16, chs)
    xsd = ctx.sb("xsd", [128, DIN], BF16, chs)
    dtc = ctx.sb("dtc", [32, 260], F32, chs)
    dta = ctx.sb("dta", [32, 260], F32, chs)
    acsc = ctx.sb("acsc", [32, CH], F32, chs)
    asc = ctx.sb("asc", [32, CH], F32, chs)
    aec = ctx.sb("aec", [32, CH], F32, chs)
    cbmt = ctx.sb("cbmt", [128, 384], BF16, chs)
    mtb = ctx.sb("mtb", [128, 8 * 384], BF16, chs)
    odc = ctx.sb("odc", [128, 64], F32, chs)
    decc = ctx.sb("decc", [128, 64], F32, chs)
    scrc = ctx.sb("scrc", [128, 1024], F32, chs)
    scrybf = scry[:].bitcast(BF16)

    ph = Ph(ctx)
    for dc in range(8):
        ph.dma("gpsimd", wbc[:, dc * 2336:(dc + 1) * 2336],
               w_in[dc * 128:(dc + 1) * 128, 2048:4384], dw)
        ph.dma("gpsimd", wdt[:, dc * 32:(dc + 1) * 32],
               w_in[dc * 128:(dc + 1) * 128, 4352:4384], dw)
    ph.dma("gpsimd", cdg[:].rearrange("p (t f) -> p t f", t=72),
           cdiag_h.rearrange("t p f -> p t f"), dw)
    ph.dma("gpsimd", dbc[:], drep[:], dw)
    for dst, src in ((cvb, convb), (dtbc, dtb), (idb, identb), (idf, identf),
                     (mskb, maskT), (onf, onesf), (cmb, cmask), (m2b, m2)):
        ph.dma("sync", dst[:], src, ds)
    for dc in range(8):
        ph.dma("gpsimd", xfm(0, dc), xinT[dc * 128:(dc + 1) * 128, 0:260], dx)
    ph.dma("sync", acolb[:], alog, ds)
    ph.wait("vector", dx)
    ph.wait("vector", ds)
    ph.wait("vector", dw)
    ph.op("vector", lambda e: e.memset(zz32[:], 0.0))
    ph.op("vector", lambda e: e.memset(runacs[:], 0.0))
    ph.op("vector", lambda e: e.memset(epsc[:], 1e-5))
    ph.op("vector", lambda e: e.memset(rst[:], 0.0))
    ph.op("vector", lambda e: e.memset(rstbf[:], 0.0), inc=ve)
    ph.wait("scalar", ds)
    ph.op("scalar", lambda e: e.activation(acolb[:], acolb[:], AF.Exp), inc=ac)
    ph.wait("vector", ac)
    ph.op("vector", lambda e: e.tensor_scalar_mul(acolb[:], acolb[:], -1.0), inc=ve)
    ph.close()

    ltc = lambda gl: (gl * 96, gl * 96 + 32, gl * 96 + 64)


    ystore = {}
    for c in range(NCH):
        par = c % 2
        ph = Ph(ctx)
        xw = dx.v
        if c + 1 < NCH:
            for dc in range(8):
                ph.dma("gpsimd", xfm(1 - par, dc),
                       xinT[dc * 128:(dc + 1) * 128, (c + 1) * 256:(c + 1) * 256 + 260], dx)
        ph.wait("tensor", dx, xw)
        if c == 0:
            ph.wait("tensor", dw, dw.v)
            ph.wait("tensor", ds, ds.v)
        ev_ip = {}
        ac_ip = {}
        for ot in range(18):
            ipb = PF[:, (ot % 2) * 512:(ot % 2) * 512 + 260]
            cvp = PF[:, 1024 + (ot % 2) * 512:1024 + (ot % 2) * 512 + 256]
            if ot >= 2:
                ph.wait("tensor", ve, ev_ip[ot - 2])
                ph.wait("tensor", ac, ac_ip[ot - 2])
            for dc in range(8):
                ph.op("tensor", lambda e, ot=ot, dc=dc, ipb=ipb, par=par: e.matmul(
                    ipb, wbc[:, dc * 2336 + ot * 128: dc * 2336 + ot * 128 + 128],
                    xfm(par, dc), start=(dc == 0), stop=(dc == 7)),
                    inc=pe if dc == 7 else None)
            ph.wait("vector", pe)
            ph.op("vector", lambda e, ot=ot, ipb=ipb: e.tensor_copy(
                xbcc[:, ot * 260:(ot + 1) * 260], ipb), inc=ve)
            ph.wait("tensor", ve)
            for k in range(4):
                ph.op("tensor", lambda e, ot=ot, k=k, cvp=cvp: e.matmul(
                    cvp, cdg[:, (ot * 4 + k) * 128:(ot * 4 + k) * 128 + 128],
                    xbcc[:, ot * 260 + 1 + k: ot * 260 + 257 + k],
                    start=(k == 0), stop=(k == 3)),
                    inc=pe if k == 3 else None)
            ph.wait("scalar", pe)
            if ot < 16:
                dst = xbcc[:, ot * 260 + 4:(ot + 1) * 260]
            elif ot == 16:
                dst = bfm[:]
            else:
                dst = csv[:, c * 256:(c + 1) * 256]
            ph.op("scalar", lambda e, cvp=cvp, dst=dst, ot=ot: e.activation(
                dst, cvp, AF.Silu, bias=cvb[:, ot:ot + 1]), inc=ac)
            ev_ip[ot] = ve.v
            ac_ip[ot] = ac.v

        dtp = PG[:, 0:260]
        for dc in range(8):
            ph.op("tensor", lambda e, dc=dc, par=par: e.matmul(
                dtp[0:32, :], wdt[:, dc * 32:(dc + 1) * 32], xfm(par, dc),
                start=(dc == 0), stop=(dc == 7)), inc=pe if dc == 7 else None)
        ph.wait("vector", pe)
        ph.op("vector", lambda e: e.tensor_scalar(
            dtc[:], dtp[0:32, :], dtbc[:, 0:1], None, ALU.add), inc=ve)
        ph.wait("scalar", ve)
        ph.op("scalar", lambda e: e.activation(dta[:], dtc[:], AF.Abs))
        ph.op("scalar", lambda e: e.activation(dta[:], dta[:], AF.Exp, scale=-1.0))
        ph.op("scalar", lambda e: e.activation(dta[:], dta[:], AF.Ln, bias=1.0), inc=ac)
        ph.wait("vector", ac)
        ph.op("vector", lambda e: e.tensor_scalar_max(dtc[:], dtc[:], 0.0))
        ph.op("vector", lambda e: e.tensor_add(dtc[:], dtc[:], dta[:]))
        ph.op("vector", lambda e: e.tensor_scalar(
            dta[:], dtc[:], acolb[:, 0:1], None, ALU.mult))
        ph.op("vector", lambda e: e.tensor_tensor_scan(
            acsc[:], dta[:, 4:260], zz32[:], runacs[:, 0:1], ALU.add, ALU.add))
        ph.op("vector", lambda e: e.tensor_scalar(
            asc[:], acsc[:], runacs[:, 0:1], None, ALU.subtract))
        ph.op("vector", lambda e: e.tensor_scalar(
            aec[:], acsc[:], acsc[:, 255:256], None, ALU.subtract))
        ph.op("vector", lambda e, c=c: e.tensor_copy(atot[:, c:c + 1], asc[:, 255:256]))
        ph.op("vector", lambda e: e.tensor_copy(runacs[:], acsc[:, 255:256]), inc=ve)

        ph.wait("tensor", ve)
        ph.wait("tensor", ac)
        for lt in range(2):
            gl = 2 * c + lt
            c0 = ltc(gl)[0]
            sl = PG[:, 512:512 + 96]
            ph.op("tensor", lambda e, lt=lt, sl=sl: e.transpose(
                sl[:, 0:32], dtc[:, 4 + lt * 128:4 + lt * 128 + 128], idf[0:32, 0:32]))
            ph.op("tensor", lambda e, lt=lt, sl=sl: e.transpose(
                sl[:, 32:64], asc[:, lt * 128:lt * 128 + 128], idf[0:32, 0:32]))
            ph.op("tensor", lambda e, lt=lt, sl=sl: e.transpose(
                sl[:, 64:96], aec[:, lt * 128:lt * 128 + 128], idf[0:32, 0:32]), inc=pe)
            ph.wait("vector", pe)
            ph.op("vector", lambda e, sl=sl, c0=c0: e.tensor_copy(
                datm[:, c0:c0 + 96], sl[:]), inc=ve)
            ph.wait("tensor", ve)

        for lt in range(2):
            ph.op("tensor", lambda e, lt=lt: e.transpose(
                PTb[:, 0:128], bfm[:, lt * 128:lt * 128 + 128], idb[:]), inc=pe)
            ph.wait("vector", pe)
            ph.op("vector", lambda e, lt=lt: e.tensor_copy(
                btm[:, lt * 128:lt * 128 + 128], PTb[:, 0:128]), inc=ve)
            ph.wait("tensor", ve)
            for half in range(2):
                base = 1024
                for f8 in range(8):
                    ft = half * 8 + f8
                    ph.op("tensor", lambda e, ft=ft, lt=lt, f8=f8, base=base: e.transpose(
                        PTb[:, base + f8 * 128: base + (f8 + 1) * 128],
                        xbcc[:, ft * 260 + 4 + lt * 128: ft * 260 + 4 + lt * 128 + 128],
                        idb[:]), inc=pe if f8 == 7 else None)
                ph.wait("vector", pe)
                ph.op("vector", lambda e, lt=lt, half=half, base=base: e.tensor_copy(
                    xstm[:, lt * 2048 + half * 1024: lt * 2048 + (half + 1) * 1024],
                    PTb[:, base:base + 1024]), inc=ve)
                ph.wait("tensor", ve)
            gl = 2 * c + lt
            c0, c1, c2 = ltc(gl)
            ph.wait("scalar", ve)
            ph.op("scalar", lambda e, c1=c1, lt=lt: e.activation(
                odc[:, lt * 32:(lt + 1) * 32], datm[:, c1:c1 + 32], AF.Exp))
            ph.op("scalar", lambda e, c2=c2, lt=lt: e.activation(
                decc[:, lt * 32:(lt + 1) * 32], datm[:, c2:c2 + 32], AF.Exp, scale=-1.0),
                inc=ac)
            xv = xstm[:, lt * 2048:(lt + 1) * 2048]
            yv = ybuf[:, (par * 2 + lt) * 2048:(par * 2 + lt + 1) * 2048]
            ph.wait("gpsimd", ac)
            ph.wait("gpsimd", st, ystore.get(c - 2, 0))
            ph.wait("gpsimd", ve)
            ph.op("gpsimd", lambda e, xv=xv, yv=yv: e.tensor_tensor(yv, xv, dbc[:], ALU.mult), inc=po)
            dtv = datm[:, c0:c0 + 32].unsqueeze(2).broadcast_to([128, 32, 64])
            xvh = xv.rearrange("p (h q) -> p h q", h=32)
            ph.wait("vector", po)
            ph.op("vector", lambda e, xvh=xvh, dtv=dtv: e.tensor_tensor(
                xvh, xvh, dtv, ALU.mult), inc=ve)
        ph.close()

        ph = Ph(ctx)
        ph.op("tensor", lambda e, c=c: e.transpose(
            PG[0:1, 0:32], atot[:, c:c + 1], idf[0:32, 0:32]), inc=pe)
        ph.wait("scalar", pe)
        ph.op("scalar", lambda e: e.activation(grow[:], PG[0:1, 0:32], AF.Exp), inc=ac)
        for stile in range(2):
            dstp = PF[:, 1024 + stile * 512: 1024 + stile * 512 + (256 if stile == 0 else 128)]
            ph.op("tensor", lambda e, stile=stile, dstp=dstp, c=c: e.matmul(
                dstp, bfm[:, stile * 128:stile * 128 + 128],
                csv[:, c * 256 + stile * 128:(c + 1) * 256],
                start=True, stop=True), inc=pe)
        ph.wait("vector", pe)
        ph.op("vector", lambda e: e.tensor_tensor(
            cbmt[:, 0:256], PF[:, 1024:1280], mskb[:], ALU.mult))
        ph.op("vector", lambda e: e.tensor_tensor(
            cbmt[:, 256:384], PF[:, 1536:1664], mskb[:, 0:128], ALU.mult), inc=ve)

        for lt in range(2):
            ph.wait("tensor", ve, ve.v)
            for ns in range(4):
                ph.op("tensor", lambda e, lt=lt, ns=ns, c=c: e.matmul(
                    PF[:, ns * 512:(ns + 1) * 512],
                    csv[:, c * 256 + lt * 128: c * 256 + lt * 128 + 128],
                    rstbf[:, ns * 512:(ns + 1) * 512],
                    start=True, stop=True), inc=pe if ns == 3 else None)
            ph.wait("vector", pe)
            odv = odc[:, lt * 32:(lt + 1) * 32].unsqueeze(2).broadcast_to([128, 32, 64])
            yv = ybuf[:, (par * 2 + lt) * 2048:(par * 2 + lt + 1) * 2048]
            ph.op("vector", lambda e, odv=odv: e.tensor_tensor(
                scrybf.rearrange("p (h q) -> p h q", h=32),
                PF[:, 0:2048].rearrange("p (h q) -> p h q", h=32), odv, ALU.mult))
            ph.wait("vector", po)
            ph.op("vector", lambda e, yv=yv: e.tensor_add(yv, yv, scrybf), inc=ve)

        ph.wait("tensor", ac)
        tsval = {}
        for hg in range(4):
            for h8 in range(8):
                h = hg * 8 + h8
                gp = h8 % 2
                r2 = PG[:, 0:256] if gp == 0 else PG[:, 512:768]
                sc = scrc[:, gp * 512: gp * 512 + 384]
                gl0, gl1 = 2 * c, 2 * c + 1
                a0c = ltc(gl0)[1]
                a1c = ltc(gl1)[1]
                if h - 2 in tsval:
                    ph.wait("tensor", ve, tsval[h - 2])
                elif h == 0:
                    ph.wait("tensor", ve, ve.v)
                ph.op("tensor", lambda e, h=h, r2=r2, a0c=a0c: e.transpose(
                    r2[:, 0:128], datm[:, a0c + h:a0c + h + 1].broadcast_to([128, 128]),
                    idf[:]))
                ph.op("tensor", lambda e, h=h, r2=r2, a1c=a1c: e.transpose(
                    r2[:, 128:256], datm[:, a1c + h:a1c + h + 1].broadcast_to([128, 128]),
                    idf[:]), inc=pe)
                ph.wait("vector", pe)
                ph.op("vector", lambda e, r2=r2, a0c=a0c, h=h, sc=sc: e.tensor_scalar(
                    sc[:, 0:256], r2, datm[:, a0c + h:a0c + h + 1], 0.0,
                    ALU.subtract, ALU.min))
                ph.op("vector", lambda e, r2=r2, a1c=a1c, h=h, sc=sc: e.tensor_scalar(
                    sc[:, 256:384], r2[:, 128:256], datm[:, a1c + h:a1c + h + 1], 0.0,
                    ALU.subtract, ALU.min), inc=ve)
                tsval[h] = ve.v
                ph.wait("scalar", ve)
                ph.op("scalar", lambda e, sc=sc: e.activation(sc[:, 0:256], sc[:, 0:256], AF.Exp))
                ph.op("scalar", lambda e, sc=sc: e.activation(sc[:, 256:384], sc[:, 256:384], AF.Exp), inc=ac)
                ph.wait("vector", ac)
                ph.op("vector", lambda e, h8=h8, sc=sc: e.tensor_tensor(
                    mtb[:, h8 * 384:h8 * 384 + 256], sc[:, 0:256], cbmt[:, 0:256], ALU.mult))
                ph.op("vector", lambda e, h8=h8, sc=sc: e.tensor_tensor(
                    mtb[:, h8 * 384 + 256:h8 * 384 + 384], sc[:, 256:384], cbmt[:, 256:384],
                    ALU.mult), inc=ve)
            ph.wait("tensor", ve)
            for lt in range(2):
                bank = PF[:, (lt * 2) * 512:(lt * 2) * 512 + 512]
                for h8 in range(8):
                    h = hg * 8 + h8
                    if lt == 0:
                        ph.op("tensor", lambda e, h8=h8, bank=bank, h=h: e.matmul(
                            bank[:, h8 * 64:(h8 + 1) * 64],
                            mtb[:, h8 * 384:h8 * 384 + 128],
                            xstm[:, h * 64:h * 64 + 64],
                            start=True, stop=True),
                            inc=pe if h8 == 7 else None)
                    else:
                        ph.op("tensor", lambda e, h8=h8, bank=bank, h=h: e.matmul(
                            bank[:, h8 * 64:(h8 + 1) * 64],
                            mtb[:, h8 * 384 + 128:h8 * 384 + 256],
                            xstm[:, h * 64:h * 64 + 64],
                            start=True, stop=False))
                        ph.op("tensor", lambda e, h8=h8, bank=bank, h=h: e.matmul(
                            bank[:, h8 * 64:(h8 + 1) * 64],
                            mtb[:, h8 * 384 + 256:h8 * 384 + 384],
                            xstm[:, 2048 + h * 64:2048 + h * 64 + 64],
                            start=False, stop=True),
                            inc=pe if h8 == 7 else None)
                ph.wait("vector", pe)
                yv = ybuf[:, (par * 2 + lt) * 2048 + hg * 512:(par * 2 + lt) * 2048 + (hg + 1) * 512]
                ph.op("vector", lambda e, yv=yv, bank=bank: e.tensor_add(yv, yv, bank), inc=ve)

        for half in range(2):
            hpc = slice(half * 1024, (half + 1) * 1024)
            sp = PG[:, 0:1024]
            for lt in range(2):
                dv = (decc[:, lt * 32 + half * 16: lt * 32 + half * 16 + 16]
                      .unsqueeze(2).broadcast_to([128, 16, 64]))
                ph.op("vector", lambda e, lt=lt, half=half, dv=dv: e.tensor_tensor(
                    xsd[:, lt * 1024:(lt + 1) * 1024].rearrange("p (h q) -> p h q", h=16),
                    xstm[:, lt * 2048 + half * 1024: lt * 2048 + (half + 1) * 1024]
                        .rearrange("p (h q) -> p h q", h=16),
                    dv, ALU.mult), inc=ve)
            ph.wait("tensor", ve)
            for ns in range(2):
                for lt in range(2):
                    ph.op("tensor", lambda e, ns=ns, lt=lt, sp=sp: e.matmul(
                        sp[:, ns * 512:(ns + 1) * 512],
                        btm[:, lt * 128:lt * 128 + 128],
                        xsd[:, lt * 1024 + ns * 512: lt * 1024 + (ns + 1) * 512],
                        start=(lt == 0), stop=(lt == 1)),
                        inc=pe if (lt == 1) else None)
            gvv = (grow[:, half * 16:(half + 1) * 16]
                   .unsqueeze(2).broadcast_to([1, 16, 64]))
            for ns in range(2):
                ph.op("tensor", lambda e, gvv=gvv, ns=ns: e.matmul(
                    PF[:, 1024 + ns * 512:1024 + (ns + 1) * 512],
                    onf[0:1, :],
                    gvv[:, ns * 8:(ns + 1) * 8, :],
                    start=True, stop=True), inc=pe if ns == 1 else None)
            ph.wait("vector", pe)
            ph.op("vector", lambda e, hpc=hpc: e.tensor_tensor(
                rst[:, hpc], rst[:, hpc], PF[:, 1024:2048], ALU.mult))
            ph.op("vector", lambda e, hpc=hpc, sp=sp: e.tensor_add(
                rst[:, hpc], rst[:, hpc], sp[:]))
            ph.op("vector", lambda e, hpc=hpc: e.tensor_copy(
                rstbf[:, hpc], rst[:, hpc]), inc=ve)
        ph.wait("sync", ve)
        for lt in range(2):
            ph.dma("sync", yhbm[(2 * c + lt) * 128:(2 * c + lt + 1) * 128, :],
                   ybuf[:, (par * 2 + lt) * 2048:(par * 2 + lt + 1) * 2048], st)
        ystore[c] = st.v
        ph.close()

        if dbg and c == 0:
            ph = Ph(ctx)
            dbg_list = [
                ("d_xbcc", xbcc, BF16), ("d_bfm", bfm, BF16), ("d_btm", btm, BF16),
                ("d_dtc", dtc, F32), ("d_acsc", acsc, F32), ("d_asc", asc, F32),
                ("d_aec", aec, F32), ("d_xstm", xstm, BF16), ("d_xsd", xsd, BF16),
                ("d_cbmt", cbmt, BF16), ("d_mtb", mtb, BF16), ("d_odc", odc, F32),
                ("d_decc", decc, F32), ("d_rst", rst, F32),
            ]
            ph.wait("gpsimd", ve)
            ph.wait("gpsimd", pe)
            ph.wait("gpsimd", ac)
            for nm, tt_, dt_ in dbg_list:
                shp = list(tt_[:].shape)
                od = nc.dram_tensor(nm, shp, F32, kind="ExternalOutput").ap()
                ph.dma("gpsimd", od, tt_[:], dx)
            odym = nc.dram_tensor("d_ybuf", [128, 2 * 2048], F32, kind="ExternalOutput").ap()
            ph.dma("gpsimd", odym, ybuf[:, 0:2 * 2048], dx)
            oddatm = nc.dram_tensor("d_datm", [128, 192], F32, kind="ExternalOutput").ap()
            ph.dma("gpsimd", oddatm, datm[:, 0:192], dx)
            ph.wait("gpsimd", dx)
            ph.close()

    chs.close()

    wz = ctx.sb("wz", [128, 8 * 2048], BF16)
    wob = ctx.sb("wob", [128, 16 * 1024], BF16)
    nrmb = ctx.sb("nrmb", [128, DIN], BF16)
    ph = Ph(ctx)
    for dc in range(8):
        ph.dma("gpsimd", wz[:, dc * 2048:(dc + 1) * 2048],
               w_in[dc * 128:(dc + 1) * 128, 0:2048], dw)
    for ic in range(16):
        ph.dma("gpsimd", wob[:, ic * 1024:(ic + 1) * 1024],
               w_out[ic * 128:(ic + 1) * 128, :], dw)
    ph.dma("gpsimd", nrmb[:], nrmrep[:], dw)
    ph.dma("sync", agsrc[:, 0:2048], rst[:], st)
    ph.dma("sync", agsrc[0:32, 2048:2049], runacs[:], st, allow_slow_non_contiguous=True)
    ph.wait("gpsimd", st)
    ph.op("gpsimd", lambda e: e.collective_compute(
        "AllGather", ALU.bypass, replica_groups=RG,
        ins=[agsrc.opt()], outs=[agdst.opt()]), inc=cc)
    ph.close()

    fxs = ExitStack()
    fld = ctx.sb("fld", [128, 2048], F32, fxs)
    agc = ctx.sb("agc", [32, 4], F32, fxs)
    rin = ctx.sb("rin", [128, 2048], F32, fxs)
    rinbf = ctx.sb("rinbf", [128, 2048], BF16, fxs)
    fcol = ctx.sb("fcol", [32, 4], F32, fxs)
    scrf = ctx.sb("scrf", [128, 1024], F32, fxs)
    ph = Ph(ctx)
    ph.wait("sync", cc, cc.v)
    for i in range(4):
        ph.dma("sync", agc[:, i:i + 1], agdst[i * 128:i * 128 + 32, 2048:2049], ds,
               allow_slow_non_contiguous=True)
    ph.wait("vector", ds)
    ph.op("vector", lambda e: e.memset(rin[:], 0.0), inc=ve)
    for i in range(3):
        if i > 0:
            ph.wait("sync", ve, ve.v)
        ph.dma("sync", fld[:], agdst[i * 128:(i + 1) * 128, 0:2048], ds)
        ph.wait("vector", ds)
        ph.op("vector", lambda e, i=i: e.tensor_scalar(
            fcol[:, 0:1], agc[:, 0:1], m2b[:, i * 4:i * 4 + 1], None, ALU.mult))
        for j in range(1, 4):
            ph.op("vector", lambda e, i=i, j=j: e.scalar_tensor_tensor(
                fcol[:, 0:1], agc[:, j:j + 1], m2b[:, i * 4 + j:i * 4 + j + 1],
                fcol[:, 0:1], ALU.mult, ALU.add))
        ph.op("vector", lambda e: e.tensor_copy(fcol[:, 1:2], fcol[:, 0:1]), inc=ve)
        ph.wait("scalar", ve)
        ph.op("scalar", lambda e: e.activation(fcol[:, 2:3], fcol[:, 1:2], AF.Exp), inc=ac)
        ph.wait("vector", ac)
        ph.op("vector", lambda e, i=i: e.tensor_scalar(
            fcol[:, 3:4], fcol[:, 2:3], cmb[:, i:i + 1], None, ALU.mult), inc=ve)
        ph.wait("tensor", ve)
        ph.op("tensor", lambda e: e.transpose(PG[0:1, 0:32], fcol[:, 3:4], idf[0:32, 0:32]), inc=pe)
        ph.wait("vector", pe)
        ph.op("vector", lambda e: e.tensor_copy(grow[:], PG[0:1, 0:32]), inc=ve)
        ph.wait("tensor", ve)
        gvv = grow[:].unsqueeze(2).broadcast_to([1, 32, 64])
        for ns in range(4):
            ph.op("tensor", lambda e, ns=ns, gvv=gvv: e.matmul(
                PF[:, ns * 512:(ns + 1) * 512], onf[0:1, :],
                gvv[:, ns * 8:(ns + 1) * 8, :], start=True, stop=True),
                inc=pe if ns == 3 else None)
        ph.wait("vector", pe)
        for hf in range(2):
            ph.op("vector", lambda e, hf=hf: e.tensor_tensor(
                scrf[:], fld[:, hf * 1024:(hf + 1) * 1024],
                PF[:, hf * 1024:(hf + 1) * 1024], ALU.mult))
            ph.op("vector", lambda e, hf=hf: e.tensor_add(
                rin[:, hf * 1024:(hf + 1) * 1024], rin[:, hf * 1024:(hf + 1) * 1024],
                scrf[:]), inc=ve)
    ph.op("vector", lambda e: e.tensor_copy(rinbf[:], rin[:]), inc=ve)
    if dbg:
        od_ = nc.dram_tensor("d_rin", [128, 2048], F32, kind="ExternalOutput").ap()
        ph.wait("gpsimd", ve)
        ph.dma("gpsimd", od_, rin[:], dx)
        odag = nc.dram_tensor("d_agdst", [512, 2052], F32, kind="ExternalOutput").ap()
        ph.dma("gpsimd", odag, agdst, dx)
    ph.close()

    odf = ctx.sb("odf", [128, 64], F32, fxs)

    eps = ExitStack()
    zsil = ctx.sb("zsil", [128, DIN], BF16, eps)
    ubuf = ctx.sb("ubuf", [128, DIN], BF16, eps)
    unfm = ctx.sb("unfm", [128, DIN], BF16, eps)
    xres = ctx.sb("xres", [128, 2 * D], BF16, eps)
    sres = ctx.sb("sres", [128, D], F32, eps)
    scre = ctx.sb("scre", [128, DIN], F32, eps)
    lncol = ctx.sb("lncol", [128, 8], F32, eps)
    x1b = ctx.sb("x1b", [128, 2 * D], BF16, eps)
    ych = ctx.sb("ych", [128, 2 * DIN], BF16, eps)
    if dbg == "epi":
        dbgu = ctx.sb("dbgu", [128, DIN], BF16, eps)
        dbgc = ctx.sb("dbgc", [128, 8], F32, eps)

    xres_rdy = {}
    ph = Ph(ctx)
    for dc in range(8):
        ph.dma("gpsimd", xfm(0, dc), xinT[dc * 128:(dc + 1) * 128, 0:260], dx)
    for tt0 in range(2):
        xres_rdy[tt0] = ph.dma("gpsimd", xres[:, (tt0 % 2) * D:(tt0 % 2 + 1) * D],
                               xin[tt0 * 128:(tt0 + 1) * 128, :], dx)
    ph.close()

    for c in range(NCH):
        par = c % 2
        ph = Ph(ctx)
        xw = dx.v
        if c + 1 < NCH:
            for dc in range(8):
                ph.dma("gpsimd", xfm(1 - par, dc),
                       xinT[dc * 128:(dc + 1) * 128, (c + 1) * 256:(c + 1) * 256 + 260], dx)
        ph.wait("tensor", dx, xw)
        ph.wait("tensor", dw, dw.v)
        ph.wait("vector", dw, dw.v)
        ph.wait("sync", ve, ve.v)
        for lt in range(2):
            yw = ph.dma("sync", ych[:, lt * 2048:(lt + 1) * 2048],
                        yhbm[(2 * c + lt) * 128:(2 * c + lt + 1) * 128, :], ds)
        ph.wait("vector", ds, yw)
        for lt in range(2):
            gl = 2 * c + lt
            c1 = ltc(gl)[1]
            ph.op("scalar", lambda e, c1=c1, lt=lt: e.activation(
                odf[:, lt * 32:(lt + 1) * 32], datm[:, c1:c1 + 32], AF.Exp), inc=ac)
        for lt in range(2):
            ph.wait("tensor", ve, ve.v)
            for ns in range(4):
                ph.op("tensor", lambda e, lt=lt, ns=ns, c=c: e.matmul(
                    PF[:, ns * 512:(ns + 1) * 512],
                    csv[:, c * 256 + lt * 128: c * 256 + lt * 128 + 128],
                    rinbf[:, ns * 512:(ns + 1) * 512],
                    start=True, stop=True), inc=pe if ns == 3 else None)
            ph.wait("vector", pe)
            ph.wait("vector", ac)
            odv = odf[:, lt * 32:(lt + 1) * 32].unsqueeze(2).broadcast_to([128, 32, 64])
            yv = ych[:, lt * 2048:(lt + 1) * 2048]
            ph.op("vector", lambda e, odv=odv: e.tensor_tensor(
                scrybf.rearrange("p (h q) -> p h q", h=32),
                PF[:, 0:2048].rearrange("p (h q) -> p h q", h=32), odv, ALU.mult))
            ph.op("vector", lambda e, yv=yv: e.tensor_add(yv, yv, scrybf), inc=ve)
        ph.wait("tensor", ve)
        ph.op("tensor", lambda e, c=c: e.transpose(PG[0:1, 0:32], atot[:, c:c + 1], idf[0:32, 0:32]), inc=pe)
        ph.wait("vector", pe)
        ph.op("vector", lambda e: e.tensor_copy(grow[:], PG[0:1, 0:32]), inc=ve)
        ph.wait("scalar", ve)
        ph.op("scalar", lambda e: e.activation(grow[:], grow[:], AF.Exp), inc=ac)
        ph.wait("tensor", ac)
        gvv = grow[:].unsqueeze(2).broadcast_to([1, 32, 64])
        for ns in range(4):
            ph.op("tensor", lambda e, ns=ns, gvv=gvv: e.matmul(
                PF[:, ns * 512:(ns + 1) * 512], onf[0:1, :],
                gvv[:, ns * 8:(ns + 1) * 8, :], start=True, stop=True),
                inc=pe if ns == 3 else None)
        ph.wait("vector", pe)
        ph.op("vector", lambda e: e.tensor_tensor(rin[:], rin[:], PF[:, 0:2048], ALU.mult))
        ph.op("vector", lambda e: e.tensor_copy(rinbf[:], rin[:]), inc=ve)
        for lt in range(2):
            tt = 2 * c + lt
            ph.wait("tensor", ve, ve.v)
            for dc in range(8):
                for ns in range(4):
                    ph.op("tensor", lambda e, dc=dc, ns=ns, lt=lt, par=par: e.matmul(
                        PF[:, ns * 512:(ns + 1) * 512],
                        xfm(par, dc)[:, 4 + lt * 128: 4 + lt * 128 + 128],
                        wz[:, dc * 2048 + ns * 512: dc * 2048 + (ns + 1) * 512],
                        start=(dc == 0), stop=(dc == 7)),
                        inc=pe if (dc == 7 and ns == 3) else None)
            ph.wait("scalar", pe)
            ph.op("scalar", lambda e: e.activation(zsil[:], PF[:, 0:2048], AF.Silu), inc=ac)
            ph.wait("vector", ac)
            yv = ych[:, lt * 2048:(lt + 1) * 2048]
            ph.op("vector", lambda e, yv=yv: e.tensor_mul(ubuf[:], yv, zsil[:]), inc=ve)
            if dbg == "epi" and tt == 21:
                ph.op("vector", lambda e: e.tensor_copy(dbgu[:], ubuf[:]), inc=ve)
            ph.wait("scalar", ve)
            ph.op("scalar", lambda e: e.activation(scre[:], ubuf[:], AF.Square,
                                                   accum_out=lncol[:, 0:1]), inc=ac)
            ph.wait("scalar", ac)
            ph.op("scalar", lambda e: e.activation(
                lncol[:, 2:3], lncol[:, 0:1], AF.Sqrt, scale=1.0 / 2048,
                bias=epsc[:, 0:1]), inc=ac)
            ph.wait("vector", ac)
            ph.op("vector", lambda e: e.reciprocal(lncol[:, 3:4], lncol[:, 2:3]), inc=ve)
            ph.wait("vector", ve)
            ph.op("vector", lambda e: e.scalar_tensor_tensor(
                ubuf[:], ubuf[:], lncol[:, 3:4], nrmb[:], ALU.mult, ALU.mult), inc=ve)
            ph.wait("tensor", ve)
            for half in range(2):
                for f8 in range(8):
                    ph.op("tensor", lambda e, half=half, f8=f8: e.transpose(
                        PTb[:, half * 1024 + f8 * 128: half * 1024 + (f8 + 1) * 128],
                        ubuf[:, half * 1024 + f8 * 128: half * 1024 + (f8 + 1) * 128],
                        idb[:]), inc=pe if f8 == 7 else None)
                ph.wait("vector", pe)
                ph.op("vector", lambda e, half=half: e.tensor_copy(
                    unfm[:, half * 1024:(half + 1) * 1024],
                    PTb[:, half * 1024:(half + 1) * 1024]), inc=ve)
            ph.wait("tensor", ve)
            ph.wait("tensor", ac)
            for ic in range(16):
                for ns in range(2):
                    ph.op("tensor", lambda e, ic=ic, ns=ns: e.matmul(
                        PF[:, ns * 512:(ns + 1) * 512],
                        unfm[:, ic * 128:(ic + 1) * 128],
                        wob[:, ic * 1024 + ns * 512: ic * 1024 + (ns + 1) * 512],
                        start=(ic == 0), stop=(ic == 15)),
                        inc=pe if (ic == 15 and ns == 1) else None)
            ph.wait("vector", pe)
            if dbg == "epi" and tt == 21:
                ph.op("vector", lambda e: e.tensor_copy(scry[:], PF[:, 0:1024]), inc=ve)
            ph.wait("vector", dx, xres_rdy[tt])
            ph.op("vector", lambda e, lt=lt: e.tensor_add(
                sres[:], PF[:, 0:1024], xres[:, lt * D:(lt + 1) * D]), inc=ve)
            ntt = tt + 2
            if ntt < NTT:
                ph.wait("gpsimd", ve)
                xres_rdy[ntt] = ph.dma("gpsimd", xres[:, (ntt % 2) * D:(ntt % 2 + 1) * D],
                                       xin[ntt * 128:(ntt + 1) * 128, :], dx)
            ph.wait("vector", st, st.v)
            ln_emit(ph, sres[:], x1b[:, lt * D:(lt + 1) * D], lncol, scre[:, 0:1024],
                    pe, ve, ac, epsc[:, 0:1],
                    snap=(dbgc[:] if (dbg == "epi" and tt == 21) else None))
            ph.wait("sync", ve)
            ph.dma("sync", x1[tt * 128:(tt + 1) * 128, :], x1b[:, lt * D:(lt + 1) * D], st)
        ph.close()
        if dbg == "epi" and c == 10:
            ph = Ph(ctx)
            for nm, tt_ in (("e_zsil", zsil), ("e_ubuf", ubuf), ("e_unfm", unfm),
                            ("e_sres", sres), ("e_ych", ych), ("e_x1b", x1b),
                            ("e_oproj", scry), ("e_xres", xres), ("e_wob", wob),
                            ("e_dbgu", dbgu), ("e_dbgc", dbgc), ("e_nrmb", nrmb)):
                od_ = nc.dram_tensor(nm, list(tt_[:].shape), F32, kind="ExternalOutput").ap()
                ph.dma("gpsimd", od_, tt_[:], dx)
            ph.wait("gpsimd", dx)
            ph.close()
    eps.close()
    fxs.close()

    ph = Ph(ctx)
    ph.wait("sync", st)
    ph.close()
    return nc, ctx



def _consts_np():
    c = {}
    c["identb"] = np.eye(128, dtype=BF)
    c["identf"] = np.eye(128, dtype=np.float32)
    s = np.arange(128)[:, None]
    l = np.arange(256)[None, :]
    c["maskT"] = (l >= s).astype(BF)
    c["onesf"] = np.ones((1, 128), np.float32)
    return c


def _core_masks(q):
    cm = np.zeros((32, 3), dtype=np.float32)
    m2 = np.zeros((32, 12), dtype=np.float32)
    for i in range(3):
        if i < q:
            cm[:, i] = 1.0
        for j in range(4):
            if i < j < q:
                m2[:, i * 4 + j] = 1.0
    return cm, m2


def prep_l0attn(inputs):
    x = inputs["L0_attn_x"]
    conv_w = np.asarray(inputs["L0_attn_conv_w"], np.float32)
    cdiag = np.zeros((72, 128, 128), np.float32)
    for ot in range(18):
        for k in range(4):
            np.fill_diagonal(cdiag[ot * 4 + k], conv_w[ot * 128:(ot + 1) * 128, k])
    convb = np.ascontiguousarray(
        np.asarray(inputs["L0_attn_conv_b"], np.float32).reshape(18, 128).T)
    cst = _consts_np()
    shared = dict(
        w_in=np.asarray(inputs["L0_attn_W_in"], np.float32),
        cdiag=cdiag, convb=convb,
        dtb=np.asarray(inputs["L0_attn_dt_bias"], np.float32).reshape(32, 1),
        alog=np.asarray(inputs["L0_attn_A_log"], np.float32).reshape(32, 1),
        drep=np.ascontiguousarray(np.tile(
            np.repeat(np.asarray(inputs["L0_attn_D"], np.float32), 64)[None, :],
            (128, 1))),
        nrmrep=np.ascontiguousarray(np.tile(
            np.asarray(inputs["L0_attn_norm_w"], np.float32)[None, :], (128, 1))),
        w_out=np.asarray(inputs["L0_attn_W_out"], np.float32),
        **cst,
    )
    in_maps = []
    for core in range(8):
        b, q = core // 4, core % 4
        t0 = q * T
        xs = np.asarray(x[b, t0:t0 + T], np.float32)
        halo = np.zeros((4, D), np.float32) if q == 0 else np.asarray(
            x[b, t0 - 4:t0], np.float32)
        xT = np.ascontiguousarray(np.concatenate([halo, xs], 0).T)
        cm, m2 = _core_masks(q)
        in_maps.append(dict(shared, xinT=xT, xin=np.ascontiguousarray(xs),
                            cmask=cm, m2=m2))
    return in_maps


def kernel_unfused(**inputs):
    res1 = run_bass_kernel_spmd(build_l0attn()[0], prep_l0attn(inputs),
                                core_ids=list(range(8)))
    x1p = [np.ascontiguousarray(res1.results[c]["x1"]) for c in range(8)]
    res2 = run_bass_kernel_spmd(build_l0ffn()[0], prep_l0ffn(inputs, x1p),
                                core_ids=list(range(8)))
    x2p = [np.ascontiguousarray(res2.results[c]["x2"]) for c in range(8)]
    res3 = run_bass_kernel_spmd(build_l1attn()[0], prep_l1attn(inputs, x2p),
                                core_ids=list(range(8)))
    x3p = [np.ascontiguousarray(res3.results[c]["x3"]) for c in range(8)]
    res4 = run_bass_kernel_spmd(build_l1ffn()[0], prep_l1ffn(inputs, x3p),
                                core_ids=list(range(8)))
    out = np.zeros((2, 16384, D), np.float32)
    for c in range(8):
        b, q = c // 4, c % 4
        out[b, q * T:(q + 1) * T] = res4.results[c]["out"]
    return out


def build_fused():
    nc = bass.Bass("TRN2", target_bir_lowering=False,
                   detect_race_conditions=False)
    ctx = Ctx(nc)
    handoff = {}

    def mkdram(prefix):
        def dram(n, s, d, k):
            if n in ("x1", "x2", "x3"):
                if n not in handoff:
                    handoff[n] = nc.dram_tensor(n, s, d).ap()
                return handoff[n]
            if n == "out":
                return nc.dram_tensor("out", s, d, kind="ExternalOutput").ap()
            return nc.dram_tensor(prefix + n, s, d, kind="ExternalInput").ap()
        return dram

    for bld, pfx in ((build_l0attn, "A_"), (build_l0ffn, "B_"),
                     (build_l1attn, "C_"), (build_l1ffn, "D_")):
        ctx.push_scope()
        bld(env=(nc, ctx, mkdram(pfx)))
        ctx.pop_scope()
    return nc, ctx


def kernel(**inputs):
    nc, _ = build_fused()
    im1 = prep_l0attn(inputs)
    im2 = prep_l0ffn(inputs, None)
    im3 = prep_l1attn(inputs, None)
    im4 = prep_l1ffn(inputs, None)
    in_maps = []
    for c in range(8):
        m = {}
        for pfx, im in (("A_", im1), ("B_", im2), ("C_", im3), ("D_", im4)):
            for k, v in im[c].items():
                if k in ("x1", "x2", "x3"):
                    continue
                m[pfx + k] = v
        in_maps.append(m)
    res = run_bass_kernel_spmd(nc, in_maps, core_ids=list(range(8)))
    out = np.zeros((2, 16384, D), np.float32)
    for c in range(8):
        b, q = c // 4, c % 4
        out[b, q * T:(q + 1) * T] = res.results[c]["out"]
    return out



def top2_emit(ph, ctx, probs, gcols, cols, pe, ve, ac, scale):
    ph.wait("vector", ve)
    ph.op("vector", lambda e: e.reduce_max(cols[:, 0:1], probs, axis=AX.X), inc=ve)
    ph.wait("vector", ve)
    ph.op("vector", lambda e: e.tensor_scalar(gcols, probs, cols[:, 0:1], None, ALU.is_lt), inc=ve)
    ph.wait("vector", ve)
    ph.op("vector", lambda e: e.tensor_mul(gcols, gcols, probs), inc=ve)
    ph.wait("vector", ve)
    ph.op("vector", lambda e: e.reduce_max(cols[:, 1:2], gcols, axis=AX.X), inc=ve)
    ph.wait("vector", ve)
    ph.op("vector", lambda e: e.tensor_scalar(gcols, probs, cols[:, 1:2], None, ALU.is_ge), inc=ve)
    ph.wait("vector", ve)
    ph.op("vector", lambda e: e.tensor_mul(gcols, gcols, probs), inc=ve)
    ph.wait("vector", ve)
    ph.op("vector", lambda e: e.tensor_scalar_mul(gcols, gcols, scale), inc=ve)


def softmax_emit(ph, probs_out, logits_psum, cols, pe, ve, ac):
    ph.op("vector", lambda e: e.reduce_max(cols[:, 0:1], logits_psum, axis=AX.X), inc=ve)
    ph.wait("vector", ve)
    ph.op("vector", lambda e: e.tensor_scalar_mul(cols[:, 1:2], cols[:, 0:1], -1.0), inc=ve)
    ph.wait("scalar", ve)
    ph.op("scalar", lambda e: e.activation(probs_out, logits_psum, AF.Exp,
                                           bias=cols[:, 1:2]), inc=ac)
    ph.wait("vector", ac)
    ph.op("vector", lambda e: e.reduce_sum(cols[:, 2:3], probs_out, axis=AX.X), inc=ve)
    ph.wait("vector", ve)
    ph.op("vector", lambda e: e.reciprocal(cols[:, 3:4], cols[:, 2:3]), inc=ve)
    ph.wait("vector", ve)
    ph.op("vector", lambda e: e.tensor_scalar(
        probs_out, probs_out, cols[:, 3:4], None, ALU.mult), inc=ve)



def build_l0ffn(nexp=18, do_gate=True, env=None):
    nc, ctx, dram = _env(env)
    ei = lambda n, s, d: dram(n, s, d, "in")
    eo = lambda n, s, d: dram(n, s, d, "out")

    x1 = ei("x1", [T, D], BF16)
    wrT = ei("wrT", [18, D, D], F32)
    brf = ei("brf", [1, 18 * D], F32)
    centT = ei("centT", [D, 16], F32)
    ebias = ei("ebias", [1, 16], F32)
    identb = ei("identb", [128, 128], BF16)
    onesbf = ei("onesbf", [1, 128], BF16)
    x2 = eo("x2", [T, D], BF16)

    TG = 1024
    NTG = 4

    idb = ctx.sb("idb", [128, 128], BF16)
    onb = ctx.sb("onb", [1, 128], BF16)
    ctb = ctx.sb("ctb", [128, 8 * 16], BF16)
    ebb = ctx.sb("ebb", [1, 16], BF16)
    brb = ctx.sb("brb", [1, 18 * D], BF16)
    epsc = ctx.sb("epsc", [128, 1], F32)
    gts = ctx.sb("gts", [128, 8 * 18], F32)
    x1tg = ctx.sb("x1tg", [128, 8 * D], BF16)
    ufm = ctx.sb("ufm", [128, 8 * TG], BF16)
    ytg = ctx.sb("ytg", [128, 8 * D], F32)
    web = ctx.sb("web", [128, 2 * 8 * D], BF16)
    sres = ctx.sb("sres", [128, D], F32)
    scre = ctx.sb("scre", [128, D], F32)
    lncol = ctx.sb("lncol", [128, 8], F32)
    prb = ctx.sb("prb", [128, 16], F32)
    x2b = ctx.sb("x2b", [128, 2 * D], BF16)

    PF = ctx.ps("PF", [128, 2048], F32)
    PG = ctx.ps("PG", [128, 1024], F32)
    PTb = ctx.ps("PTb", [128, 2048], BF16)

    dw = ctx.sem("dw")
    dx = ctx.sem("dx")
    ds = ctx.sem("ds")
    pe = ctx.sem("pe")
    ve = ctx.sem("ve")
    ac = ctx.sem("ac")
    st = ctx.sem("st")

    ph = Ph(ctx)
    ph.dma("gpsimd", ctb[:].rearrange("p (dc o) -> p dc o", dc=8),
           centT.rearrange("(dc p) o -> p dc o", p=128), dw)
    ph.dma("gpsimd", ebb[:], ebias, dw)
    ph.dma("gpsimd", brb[:], brf, dw)
    ph.dma("sync", idb[:], identb, ds)
    ph.dma("sync", onb[:], onesbf, ds)
    for dc in range(8):
        ph.dma("gpsimd", web[:, dc * D:(dc + 1) * D],
               wrT[0, dc * 128:(dc + 1) * 128, :], dw)
    ph.wait("vector", ds)
    ph.op("vector", lambda e: e.memset(epsc[:], 1e-5), inc=ve)
    ph.close()

    for tg in range(4):
        ph = Ph(ctx)
        xw = ds.v
        for t8 in range(8):
            xw = ph.dma("sync", x1tg[:, t8 * D:(t8 + 1) * D],
                        x1[(tg * 8 + t8) * 128:(tg * 8 + t8 + 1) * 128, :], ds)
        ph.wait("tensor", ds, xw)
        if tg == 0:
            ph.wait("tensor", dw, dw.v)
        for t8 in range(8):
            for half in range(2):
                for f8 in range(4):
                    ph.op("tensor", lambda e, t8=t8, half=half, f8=f8: e.transpose(
                        PTb[:, half * 1024 + f8 * 128: half * 1024 + (f8 + 1) * 128],
                        x1tg[:, t8 * D + half * 512 + f8 * 128: t8 * D + half * 512 + (f8 + 1) * 128],
                        idb[:]), inc=pe if f8 == 3 else None)
                ph.wait("vector", pe)
                for f8 in range(4):
                    fidx = half * 4 + f8
                    ph.op("vector", lambda e, t8=t8, fidx=fidx, half=half, f8=f8: e.tensor_copy(
                        ufm[:, fidx * TG + t8 * 128: fidx * TG + (t8 + 1) * 128],
                        PTb[:, half * 1024 + f8 * 128: half * 1024 + (f8 + 1) * 128]),
                        inc=ve if f8 == 3 else None)
                ph.wait("tensor", ve)
        for t8 in range(8 if do_gate else 0):
            ph.wait("tensor", ve, ve.v)
            for dc in range(8):
                ph.op("tensor", lambda e, t8=t8, dc=dc: e.matmul(
                    PG[:, 0:16], ufm[:, dc * TG + t8 * 128: dc * TG + (t8 + 1) * 128],
                    ctb[:, dc * 16:(dc + 1) * 16], start=(dc == 0), stop=False))
            ph.op("tensor", lambda e, t8=t8: e.matmul(
                PG[:, 0:16], onb[0:1, 0:128], ebb[0:1, :], start=False, stop=True), inc=pe)
            ph.wait("vector", pe)
            ph.wait("scalar", pe)
            softmax_emit(ph, prb[:], PG[:, 0:16], lncol, pe, ve, ac)
            ph.wait("vector", ve)
            top2_emit(ph, ctx, prb[:], gts[:, t8 * 18:t8 * 18 + 16], lncol[:, 4:8],
                      pe, ve, ac, 1.0)
            ph.op("vector", lambda e, t8=t8: e.memset(gts[:, t8 * 18 + 16:t8 * 18 + 18], 0.5), inc=ve)
        ph.op("vector", lambda e: e.memset(ytg[:], 0.0), inc=ve)
        ph.close()

        for e_i in range(nexp):
            ph = Ph(ctx)
            par = e_i % 2
            wcur = lambda dc, par=par: web[:, par * 8 * D + dc * D:(par) * 8 * D + (dc + 1) * D]
            wv = dw.v
            ne, ntg = (e_i + 1, tg) if e_i + 1 < nexp else (0, tg + 1)
            if ntg < 4:
                for dc in range(8):
                    ph.dma("gpsimd", web[:, (1 - par) * 8 * D + dc * D:(1 - par) * 8 * D + (dc + 1) * D],
                           wrT[ne, dc * 128:(dc + 1) * 128, :], dw)
            ph.wait("tensor", dw, wv)
            ev_e = {}
            for t8 in range(8):
                bank = (t8 % 2) * 1024
                if t8 >= 2:
                    ph.wait("tensor", ve, ev_e[t8 - 2])
                for dc in range(8):
                    for ns in range(2):
                        ph.op("tensor", lambda e, t8=t8, dc=dc, ns=ns, bank=bank, par=par: e.matmul(
                            PF[:, bank + ns * 512: bank + (ns + 1) * 512],
                            ufm[:, dc * TG + t8 * 128: dc * TG + (t8 + 1) * 128],
                            wcur(dc)[:, ns * 512:(ns + 1) * 512],
                            start=(dc == 0), stop=False))
                for ns in range(2):
                    ph.op("tensor", lambda e, ns=ns, bank=bank, e_i=e_i: e.matmul(
                        PF[:, bank + ns * 512: bank + (ns + 1) * 512],
                        onb[0:1, 0:128],
                        brb[0:1, e_i * D + ns * 512: e_i * D + (ns + 1) * 512],
                        start=False, stop=True), inc=pe if ns == 1 else None)
                ph.wait("vector", pe)
                gcol = gts[:, t8 * 18 + e_i: t8 * 18 + e_i + 1]
                ph.op("vector", lambda e, t8=t8, bank=bank, gcol=gcol: e.scalar_tensor_tensor(
                    ytg[:, t8 * D:(t8 + 1) * D], PF[:, bank:bank + 1024], gcol,
                    ytg[:, t8 * D:(t8 + 1) * D], ALU.mult, ALU.add), inc=ve)
                ev_e[t8] = ve.v
            ph.close()

        ph = Ph(ctx)
        for t8 in range(8):
            tt = tg * 8 + t8
            ph.wait("vector", ac)
            ph.op("vector", lambda e, t8=t8: e.scalar_tensor_tensor(
                sres[:], x1tg[:, t8 * D:(t8 + 1) * D], 2.0,
                ytg[:, t8 * D:(t8 + 1) * D], ALU.mult, ALU.add), inc=ve)
            ph.wait("vector", st, st.v)
            ln_emit(ph, sres[:], x2b[:, (t8 % 2) * D:(t8 % 2 + 1) * D], lncol, scre[:],
                    pe, ve, ac, epsc[:, 0:1])
            ph.wait("sync", ve)
            ph.dma("sync", x2[tt * 128:(tt + 1) * 128, :], x2b[:, (t8 % 2) * D:(t8 % 2 + 1) * D], st)
        ph.close()

    ph = Ph(ctx)
    ph.wait("sync", st)
    ph.close()
    return nc, ctx


def prep_l0ffn(inputs, x1_parts):
    Wr = np.asarray(inputs["L0_ffn_Wr"], np.float32)
    Ws = np.asarray(inputs["L0_ffn_Ws"], np.float32)
    br = np.asarray(inputs["L0_ffn_br"], np.float32)
    bs = np.asarray(inputs["L0_ffn_bs"], np.float32)
    wrT = np.concatenate([Wr.transpose(0, 2, 1), Ws.transpose(0, 2, 1)], 0)
    wrT = np.ascontiguousarray(wrT)
    brf = np.concatenate([br, bs], 0).reshape(1, 18 * D)
    centT = np.ascontiguousarray(np.asarray(inputs["L0_ffn_centroids"], np.float32).T)
    ebias = np.asarray(inputs["L0_ffn_expert_biases"], np.float32).reshape(1, 16)
    shared = dict(wrT=wrT, brf=np.ascontiguousarray(brf), centT=centT, ebias=ebias,
                  identb=np.eye(128, dtype=BF), onesbf=np.ones((1, 128), BF))
    if x1_parts is None:
        return [dict(shared) for c in range(8)]
    return [dict(shared, x1=x1_parts[c]) for c in range(8)]



def build_l1attn(mode='full', nchunk=8, qk=True, chop=99, env=None):
    nc, ctx, dram = _env(env)
    ei = lambda n, s, d: dram(n, s, d, "in")
    eo = lambda n, s, d: dram(n, s, d, "out")

    x2 = ei("x2", [T, D], BF16)
    wqT = ei("wqT", [D, D], F32)
    wkT = ei("wkT", [D, D], F32)
    wvT = ei("wvT", [D, D], F32)
    woT = ei("woT", [D, D], F32)
    bq_pt = ei("bq_pt", [128, 8], F32)
    bk_pt = ei("bk_pt", [128, 8], F32)
    bv_row = ei("bv_row", [1, D], F32)
    bo_row = ei("bo_row", [1, D], F32)
    bk_row = ei("bk_row", [1, D], F32)
    identb = ei("identb", [128, 128], BF16)
    onesbf = ei("onesbf", [1, 128], BF16)
    ones128 = ei("ones128", [128, 1], BF16)
    bones = ei("bones", [128, 2], BF16)
    cm128 = ei("cm128", [128, 3], F32)
    trimask = ei("trimask", [128, 128], BF16)
    identf2 = ei("identf2", [128, 128], F32)
    x3 = eo("x3", [T, D], BF16)

    kcd = nc.dram_tensor("kcd", [1, 1024], F32).ap()
    agsrc = nc.dram_tensor("agsrc2", [128, 520], F32).ap()
    agdst = nc.dram_tensor("agdst2", [512, 520], F32).ap()

    TG = 1024

    idb = ctx.sb("idb", [128, 128], BF16)
    onb = ctx.sb("onb", [1, 128], BF16)
    on128 = ctx.sb("on128", [128, 1], BF16)
    bon = ctx.sb("bon", [128, 2], BF16)
    cmb = ctx.sb("cmb", [128, 3], F32)
    epsc = ctx.sb("epsc", [128, 1], F32)
    wqtb = ctx.sb("wqtb", [128, 8 * D], BF16)
    wktb = ctx.sb("wktb", [128, 8 * D], BF16)
    wvtb = ctx.sb("wvtb", [128, 8 * D], BF16)
    wotb = ctx.sb("wotb", [128, 8 * D], BF16)
    bqc = ctx.sb("bqc", [128, 8], F32)
    bkc = ctx.sb("bkc", [128, 8], F32)
    bvr = ctx.sb("bvr", [1, D], BF16)
    bor = ctx.sb("bor", [1, D], BF16)
    bkr = ctx.sb("bkr", [1, D], BF16)
    x2tg = ctx.sb("x2tg", [128, 8 * D], BF16)
    xfm = ctx.sb("xfm", [128, 8 * TG], BF16)
    qfm = ctx.sb("qfm", [128, 8 * TG], BF16)
    kfm = ctx.sb("kfm", [128, 8 * TG], BF16)
    ktm = ctx.sb("ktm", [128, D], BF16)
    vtm = ctx.sb("vtm", [128, D], BF16)
    mtb2 = ctx.sb("mtb2", [128, 16 * 128], BF16)
    atm = ctx.sb("atm", [128, D], BF16)
    attfm = ctx.sb("attfm", [128, D], BF16)
    ssb = ctx.sb("ssb", [128, 512], F32)
    srbf = ctx.sb("srbf", [128, 512], BF16)
    kcr = ctx.sb("kcr", [128, 8], F32)
    kcs = ctx.sb("kcs", [128, 8 * 128], F32)
    kcrow = ctx.sb("kcrow", [1, D], F32)
    pq = ctx.sb("pq", [128, 128], BF16)
    denb = ctx.sb("denb", [128, 16], F32)
    rden = ctx.sb("rden", [128, 16], F32)
    scr = ctx.sb("scr", [128, D], F32)
    sres = ctx.sb("sres", [128, D], F32)
    scre = ctx.sb("scre", [128, D], F32)
    lncol = ctx.sb("lncol", [128, 8], F32)
    x3b = ctx.sb("x3b", [128, 2 * D], BF16)
    fld = ctx.sb("fld", [128, 520], F32)
    trib = ctx.sb("trib", [128, 128], BF16)
    idf2 = ctx.sb("idf2", [128, 128], F32)
    zz128 = ctx.sb("zz128", [128, 128], F32)

    PF = ctx.ps("PF", [128, 2048], F32)
    PG = ctx.ps("PG", [128, 1024], F32)
    PTb = ctx.ps("PTb", [128, 2048], BF16)

    dw = ctx.sem("dw")
    dx = ctx.sem("dx")
    ds = ctx.sem("ds")
    pe = ctx.sem("pe")
    ve = ctx.sem("ve")
    ac = ctx.sem("ac")
    st = ctx.sem("st")
    cc = ctx.sem("cc")

    ph = Ph(ctx)
    for w, t_ in ((wqT, wqtb), (wkT, wktb), (wvT, wvtb), (woT, wotb)):
        for dc in range(8):
            ph.dma("gpsimd", t_[:, dc * D:(dc + 1) * D],
                   w[dc * 128:(dc + 1) * 128, :], dw)
    ph.dma("gpsimd", bvr[:], bv_row, dw)
    ph.dma("gpsimd", bor[:], bo_row, dw)
    ph.dma("gpsimd", bkr[:], bk_row, dw)
    for dst, srcw in ((idb, identb), (onb, onesbf), (on128, ones128), (bon, bones),
                      (cmb, cm128), (bqc, bq_pt), (bkc, bk_pt), (trib, trimask),
                      (idf2, identf2)):
        ph.dma("sync", dst[:], srcw, ds)
    ph.wait("vector", ds)
    ph.op("vector", lambda e: e.memset(epsc[:], 1e-5))
    ph.op("vector", lambda e: e.memset(ssb[:], 0.0))
    ph.op("vector", lambda e: e.memset(kcr[:], 0.0))
    ph.op("vector", lambda e: e.memset(kcrow[:], 0.0))
    ph.op("vector", lambda e: e.memset(zz128[:], 0.0), inc=ve)
    ph.close()

    def load_transpose_xtg(ph, tg):
        xw = ds.v
        for t8 in range(8):
            xw = ph.dma("sync", x2tg[:, t8 * D:(t8 + 1) * D],
                        x2[(tg * 8 + t8) * 128:(tg * 8 + t8 + 1) * 128, :], ds)
        ph.wait("tensor", ds, xw)
        for t8 in range(8):
            for half in range(2):
                for f8 in range(4):
                    ph.op("tensor", lambda e, t8=t8, half=half, f8=f8: e.transpose(
                        PTb[:, half * 1024 + f8 * 128: half * 1024 + (f8 + 1) * 128],
                        x2tg[:, t8 * D + half * 512 + f8 * 128: t8 * D + half * 512 + (f8 + 1) * 128],
                        idb[:]), inc=pe if f8 == 3 else None)
                ph.wait("vector", pe)
                for f8 in range(4):
                    fidx = half * 4 + f8
                    ph.op("vector", lambda e, t8=t8, fidx=fidx, half=half, f8=f8: e.tensor_copy(
                        xfm[:, fidx * TG + t8 * 128: fidx * TG + (t8 + 1) * 128],
                        PTb[:, half * 1024 + f8 * 128: half * 1024 + (f8 + 1) * 128]),
                        inc=ve if f8 == 3 else None)
                ph.wait("tensor", ve)

    def emit_tm_proj(ph, t8, wtb, brow, dst, elu):
        ph.wait("tensor", ve, ve.v)
        for dc in range(8):
            for ns in range(2):
                ph.op("tensor", lambda e, dc=dc, ns=ns, t8=t8, wtb=wtb: e.matmul(
                    PF[:, 1024 + ns * 512: 1024 + (ns + 1) * 512],
                    xfm[:, dc * TG + t8 * 128: dc * TG + (t8 + 1) * 128],
                    wtb[:, dc * D + ns * 512: dc * D + (ns + 1) * 512],
                    start=(dc == 0), stop=False))
        for ns in range(2):
            ph.op("tensor", lambda e, ns=ns, brow=brow: e.matmul(
                PF[:, 1024 + ns * 512: 1024 + (ns + 1) * 512],
                onb[0:1, 0:128], brow[0:1, ns * 512:(ns + 1) * 512],
                start=False, stop=True), inc=pe if ns == 1 else None)
        ph.wait("vector", pe)
        if elu:
            ph.op("vector", lambda e, dst=dst: e.tensor_scalar_min(scr[:], PF[:, 1024:2048], 0.0), inc=ve)
            ph.wait("scalar", ve)
            ph.op("scalar", lambda e: e.activation(scr[:], scr[:], AF.Exp), inc=ac)
            ph.wait("vector", ac)
            ph.op("vector", lambda e, dst=dst: e.scalar_tensor_tensor(
                dst, PF[:, 1024:2048], 0.0, scr[:], ALU.max, ALU.add), inc=ve)
        else:
            ph.op("vector", lambda e, dst=dst: e.tensor_copy(dst, PF[:, 1024:2048]), inc=ve)

    for tg in range(4 if mode != 'p2' else 0):
        ph = Ph(ctx)
        load_transpose_xtg(ph, tg)
        for t8 in range(8):
            gt = tg * 8 + t8
            emit_tm_proj(ph, t8, wktb, bkr, ktm[:], elu=True)
            emit_tm_proj(ph, t8, wvtb, bvr, vtm[:], elu=False)
            ph.wait("tensor", ve)
            for ns in range(2):
                ph.op("tensor", lambda e, ns=ns: e.matmul(
                    PG[0:1, ns * 512:(ns + 1) * 512], on128[:, 0:1],
                    ktm[:, ns * 512:(ns + 1) * 512],
                    start=True, stop=True), inc=pe if ns == 1 else None)
            ph.wait("vector", pe)
            ph.op("vector", lambda e: e.tensor_add(kcrow[:], kcrow[:], PG[0:1, 0:1024]), inc=ve)
            for h in range(16):
                ph.op("tensor", lambda e, h=h: e.matmul(
                    PF[(h % 2) * 64:(h % 2) * 64 + 64, (h // 2) * 64:(h // 2) * 64 + 64],
                    ktm[:, h * 64:(h + 1) * 64], vtm[:, h * 64:(h + 1) * 64],
                    start=True, stop=True), inc=pe if h == 15 else None)
            ph.wait("vector", pe)
            ph.op("vector", lambda e: e.tensor_add(ssb[:], ssb[:], PF[:, 0:512]), inc=ve)
        ph.close()

    if mode == 'p2':
        ph = Ph(ctx)
        ph.op("vector", lambda e: e.memset(srbf[:], 0.0), inc=ve)
        ph.close()
    ph = Ph(ctx) if mode != 'p2' else None
    if ph is not None:
        ph.wait("sync", ve)
        ph.dma("sync", kcd, kcrow[:], ds)
        ph.wait("sync", ds)
        ph.dma("sync", kcr[:], kcd.rearrange("o (f p) -> (o p) f", p=128), ds,
               allow_slow_non_contiguous=True)
        ph.wait("sync", ds)
        ph.dma("sync", agsrc[:, 0:512], ssb[:], st)
        ph.dma("sync", agsrc[:, 512:520], kcr[:], st)
        ph.wait("gpsimd", st)
        ph.op("gpsimd", lambda e: e.collective_compute(
            "AllGather", ALU.bypass, replica_groups=RG,
            ins=[agsrc.opt()], outs=[agdst.opt()]), inc=cc)
        ph.close()

    ph = Ph(ctx)
    if mode != 'p2':
        ph.wait("sync", cc, cc.v)
    ph.op("vector", lambda e: e.memset(ssb[:], 0.0))
    ph.op("vector", lambda e: e.memset(kcr[:], 0.0), inc=ve)
    for i in range(3 if mode != 'p2' else 0):
        if i > 0:
            ph.wait("sync", ve, ve.v)
        ph.dma("sync", fld[:], agdst[i * 128:(i + 1) * 128, :], ds)
        ph.wait("vector", ds)
        ph.op("vector", lambda e, i=i: e.tensor_scalar(
            scr[:, 0:520], fld[:], cmb[:, i:i + 1], None, ALU.mult), inc=ve)
        ph.wait("vector", ve)
        ph.op("vector", lambda e: e.tensor_add(ssb[:], ssb[:], scr[:, 0:512]))
        ph.op("vector", lambda e: e.tensor_add(kcr[:], kcr[:], scr[:, 512:520]), inc=ve)
    ph.op("vector", lambda e: e.tensor_copy(srbf[:], ssb[:]), inc=ve)
    ph.close()

    for tg in range(4 if mode != 'p1' else 0):
        ph = Ph(ctx)
        load_transpose_xtg(ph, tg)
        for (wtb, bcol, dstb) in (((wqtb, bqc, qfm), (wktb, bkc, kfm)) if qk else ()):
            for oc in range(8):
                for tns in range(2):
                    ph.wait("tensor", ve, ve.v)
                    for dc in range(8):
                        ph.op("tensor", lambda e, oc=oc, tns=tns, dc=dc, wtb=wtb: e.matmul(
                            PF[:, 1024:1536],
                            wtb[:, dc * D + oc * 128: dc * D + oc * 128 + 128],
                            xfm[:, dc * TG + tns * 512: dc * TG + (tns + 1) * 512],
                            start=(dc == 0), stop=(dc == 7)), inc=pe if dc == 7 else None)
                    ph.wait("vector", pe)
                    ph.op("vector", lambda e, oc=oc, bcol=bcol: e.tensor_scalar(
                        scr[:, 0:512], PF[:, 1024:1536], bcol[:, oc:oc + 1], 0.0,
                        ALU.add, ALU.min), inc=ve)
                    ph.wait("scalar", ve)
                    ph.op("scalar", lambda e: e.activation(scr[:, 0:512], scr[:, 0:512], AF.Exp), inc=ac)
                    ph.op("vector", lambda e, oc=oc, bcol=bcol: e.tensor_scalar(
                        scr[:, 512:1024], PF[:, 1024:1536], bcol[:, oc:oc + 1], 0.0,
                        ALU.add, ALU.max), inc=ve)
                    ph.wait("vector", ac)
                    ph.wait("vector", ve)
                    ph.op("vector", lambda e, oc=oc, tns=tns, dstb=dstb: e.tensor_add(
                        dstb[:, oc * TG + tns * 512: oc * TG + (tns + 1) * 512],
                        scr[:, 0:512], scr[:, 512:1024]), inc=ve)
        ph.close()

        for t8 in range(nchunk):
            ph = Ph(ctx)
            emit_tm_proj(ph, t8, wktb, bkr, ktm[:], elu=True)
            emit_tm_proj(ph, t8, wvtb, bvr, vtm[:], elu=False)
            if chop < 1:
                ph.close()
                continue
            hsl = lambda buf, h, t8=t8: buf[(h % 2) * 64:(h % 2) * 64 + 64,
                                            (h // 2) * TG + t8 * 128:(h // 2) * TG + (t8 + 1) * 128]
            ph.wait("tensor", ve)
            horder = list(range(0, 16, 2)) + list(range(1, 16, 2))
            for hi, h in enumerate(horder):
                ph.op("tensor", lambda e, h=h: e.matmul(
                    PF[:, h * 128:(h + 1) * 128], hsl(kfm, h), hsl(qfm, h),
                    start=True, stop=True), inc=pe if hi == 15 else None)
            ph.wait("vector", pe)
            for h in range(16):
                ph.op("vector", lambda e, h=h: e.tensor_tensor(
                    mtb2[:, h * 128:(h + 1) * 128], PF[:, h * 128:(h + 1) * 128],
                    trib[:], ALU.mult), inc=ve if h == 15 else None)
            if chop < 2:
                ph.close()
                continue
            for f8 in range(8):
                ph.op("vector", lambda e, f8=f8, t8=t8: e.tensor_tensor_scan(
                    kcs[:, f8 * 128:(f8 + 1) * 128],
                    kfm[:, f8 * TG + t8 * 128: f8 * TG + (t8 + 1) * 128],
                    zz128[:], kcr[:, f8:f8 + 1], ALU.add, ALU.add), inc=ve)
                ph.wait("vector", ve)
                ph.op("vector", lambda e, f8=f8: e.tensor_copy(
                    kcr[:, f8:f8 + 1], kcs[:, f8 * 128 + 127:(f8 + 1) * 128]), inc=ve)
                ph.wait("vector", ve)
                ph.op("vector", lambda e, f8=f8, t8=t8: e.tensor_tensor(
                    pq[:], qfm[:, f8 * TG + t8 * 128: f8 * TG + (t8 + 1) * 128],
                    kcs[:, f8 * 128:(f8 + 1) * 128], ALU.mult), inc=ve)
                ph.wait("tensor", ve)
                if f8 == 0:
                    ph.wait("tensor", ac, ac.v)
                ph.op("tensor", lambda e, f8=f8: e.matmul(
                    PG[:, 512 + f8 * 2:512 + f8 * 2 + 2], pq[:], bon[:],
                    start=True, stop=True), inc=pe)
            ph.wait("vector", pe)
            if chop < 3:
                ph.close()
                continue
            ph.op("vector", lambda e: e.tensor_scalar_max(denb[:], PG[:, 512:528], 1e-6), inc=ve)
            ph.wait("vector", ve)
            ph.op("vector", lambda e: e.reciprocal(rden[:], denb[:]), inc=ve)
            if chop < 4:
                ph.close()
                continue
            ph.wait("tensor", ve)
            for h in range(16):
                ph.op("tensor", lambda e, h=h: e.matmul(
                    PF[:, h * 64:(h + 1) * 64], mtb2[:, h * 128:(h + 1) * 128],
                    vtm[:, h * 64:(h + 1) * 64], start=True, stop=False))
                ph.op("tensor", lambda e, h=h: e.matmul(
                    PF[:, h * 64:(h + 1) * 64], hsl(qfm, h),
                    srbf[(h % 2) * 64:(h % 2) * 64 + 64, (h // 2) * 64:(h // 2) * 64 + 64],
                    start=False, stop=True), inc=pe if h == 15 else None)
            ph.wait("vector", pe)
            if chop < 5:
                ph.close()
                continue
            rdv = rden[:].unsqueeze(2).broadcast_to([128, 16, 64])
            ph.op("vector", lambda e, rdv=rdv: e.tensor_tensor(
                atm[:].rearrange("p (h q) -> p h q", h=16),
                PF[:, 0:1024].rearrange("p (h q) -> p h q", h=16), rdv, ALU.mult), inc=ve)
            if chop < 6:
                ph.close()
                continue
            for h in range(16):
                ph.op("tensor", lambda e, h=h: e.matmul(
                    PG[(h % 2) * 64:(h % 2) * 64 + 64, (h // 2) * 64:(h // 2) * 64 + 64],
                    ktm[:, h * 64:(h + 1) * 64], vtm[:, h * 64:(h + 1) * 64],
                    start=True, stop=True), inc=pe if h == 15 else None)
            ph.wait("vector", pe)
            ph.op("vector", lambda e: e.tensor_add(ssb[:], ssb[:], PG[:, 0:512]), inc=ve)
            ph.wait("vector", ve)
            ph.op("vector", lambda e: e.tensor_copy(srbf[:], ssb[:]), inc=ve)
            if chop < 7:
                ph.close()
                continue
            ph.wait("tensor", ve)
            for f8 in range(8):
                ph.op("tensor", lambda e, f8=f8: e.transpose(
                    PTb[:, f8 * 128:(f8 + 1) * 128], atm[:, f8 * 128:(f8 + 1) * 128],
                    idb[:]), inc=pe if f8 == 7 else None)
            ph.wait("vector", pe)
            ph.op("vector", lambda e: e.tensor_copy(attfm[:], PTb[:, 0:1024]), inc=ve)
            if chop < 8:
                ph.close()
                continue
            ph.wait("tensor", ve)
            for ic in range(8):
                for ns in range(2):
                    ph.op("tensor", lambda e, ic=ic, ns=ns: e.matmul(
                        PF[:, 1024 + ns * 512:1024 + (ns + 1) * 512],
                        attfm[:, ic * 128:(ic + 1) * 128],
                        wotb[:, ic * D + ns * 512: ic * D + (ns + 1) * 512],
                        start=(ic == 0), stop=False))
            for ns in range(2):
                ph.op("tensor", lambda e, ns=ns: e.matmul(
                    PF[:, 1024 + ns * 512:1024 + (ns + 1) * 512],
                    onb[0:1, 0:128], bor[0:1, ns * 512:(ns + 1) * 512],
                    start=False, stop=True), inc=pe if ns == 1 else None)
            ph.wait("vector", pe)
            ph.wait("vector", ac)
            ph.op("vector", lambda e, t8=t8: e.tensor_add(
                sres[:], PF[:, 1024:2048], x2tg[:, t8 * D:(t8 + 1) * D]), inc=ve)
            ph.wait("vector", st, st.v)
            ln_emit(ph, sres[:], x3b[:, (t8 % 2) * D:(t8 % 2 + 1) * D], lncol, scre[:],
                    pe, ve, ac, epsc[:, 0:1])
            ph.wait("sync", ve)
            tt = tg * 8 + t8
            ph.dma("sync", x3[tt * 128:(tt + 1) * 128, :], x3b[:, (t8 % 2) * D:(t8 % 2 + 1) * D], st)
            ph.close()

    ph = Ph(ctx)
    if mode == 'p1':
        ph.dma("gpsimd", x3[0:128, 0:8], kcr[:], st)
    ph.wait("sync", st)
    ph.close()
    return nc, ctx


def prep_l1attn(inputs, x2_parts):
    tM = np.tril(np.ones((128, 128), np.float32))
    s_ = np.arange(128)[:, None]
    l_ = np.arange(128)[None, :]
    trimask = (l_ >= s_).astype(BF)
    bones = np.zeros((128, 2), BF)
    bones[0:64, 0] = 1
    bones[64:128, 1] = 1
    shared = dict(
        wqT=np.ascontiguousarray(np.asarray(inputs["L1_attn_Wq"], np.float32).T),
        wkT=np.ascontiguousarray(np.asarray(inputs["L1_attn_Wk"], np.float32).T),
        wvT=np.ascontiguousarray(np.asarray(inputs["L1_attn_Wv"], np.float32).T),
        woT=np.ascontiguousarray(np.asarray(inputs["L1_attn_Wo"], np.float32).T),
        bq_pt=np.ascontiguousarray(np.asarray(inputs["L1_attn_bq"], np.float32).reshape(8, 128).T),
        bk_pt=np.ascontiguousarray(np.asarray(inputs["L1_attn_bk"], np.float32).reshape(8, 128).T),
        bv_row=np.asarray(inputs["L1_attn_bv"], np.float32).reshape(1, D),
        bo_row=np.asarray(inputs["L1_attn_bo"], np.float32).reshape(1, D),
        bk_row=np.asarray(inputs["L1_attn_bk"], np.float32).reshape(1, D),
        identb=np.eye(128, dtype=BF), onesbf=np.ones((1, 128), BF),
        ones128=np.ones((128, 1), BF), bones=bones, trimask=trimask,
        identf2=np.eye(128, dtype=np.float32),
    )
    out = []
    for c in range(8):
        q = c % 4
        cm = np.zeros((128, 3), np.float32)
        for i in range(3):
            if i < q:
                cm[:, i] = 1.0
        out.append(dict(shared, cm128=cm) if x2_parts is None else dict(shared, x2=x2_parts[c], cm128=cm))
    return out



def build_l1ffn(env=None):
    nc, ctx, dram = _env(env)
    ei = lambda n, s, d: dram(n, s, d, "in")
    eo = lambda n, s, d: dram(n, s, d, "out")

    x3 = ei("x3", [T, D], BF16)
    gwT = ei("gwT", [D, 8], F32)
    w1e = ei("w1e", [9, D, 4096], F32)
    w3e = ei("w3e", [9, D, 4096], F32)
    w2e = ei("w2e", [9, 4096, D], F32)
    identb = ei("identb", [128, 128], BF16)
    out = eo("out", [T, D], F32)

    TG = 1024

    idb = ctx.sb("idb", [128, 128], BF16)
    gwb = ctx.sb("gwb", [128, 8 * 8], BF16)
    epsc = ctx.sb("epsc", [128, 1], F32)
    gts = ctx.sb("gts", [128, 8 * 9], F32)
    x3tg = ctx.sb("x3tg", [128, 8 * D], BF16)
    ufm = ctx.sb("ufm", [128, 8 * TG], BF16)
    ytg = ctx.sb("ytg", [128, 8 * D], F32)
    hbuf = ctx.sb("hbuf", [128, 16 * TG], BF16)
    w13b = ctx.sb("w13b", [128, 2 * 2 * 4096], BF16)
    w2b = ctx.sb("w2b", [128, 16 * D], BF16)
    prb = ctx.sb("prb", [128, 8], F32)
    scr = ctx.sb("scr", [128, 512], F32)
    sres = ctx.sb("sres", [128, D], F32)
    scre = ctx.sb("scre", [128, D], F32)
    lncol = ctx.sb("lncol", [128, 8], F32)
    outb = ctx.sb("outb", [128, 2 * D], F32)

    PF = ctx.ps("PF", [128, 2048], F32)
    PG = ctx.ps("PG", [128, 1024], F32)
    PTb = ctx.ps("PTb", [128, 2048], BF16)

    dw = ctx.sem("dw")
    ds = ctx.sem("ds")
    pe = ctx.sem("pe")
    ve = ctx.sem("ve")
    ac = ctx.sem("ac")
    st = ctx.sem("st")

    ph = Ph(ctx)
    ph.dma("gpsimd", gwb[:].rearrange("p (dc o) -> p dc o", dc=8),
           gwT.rearrange("(dc p) o -> p dc o", p=128), dw)
    ph.dma("sync", idb[:], identb, ds)
    for dc in range(8):
        ph.dma("gpsimd", w13b[:, dc * 512:(dc + 1) * 512],
               w1e[0, dc * 128:(dc + 1) * 128, 0:512], dw)
        ph.dma("gpsimd", w13b[:, 4096 + dc * 512:4096 + (dc + 1) * 512],
               w3e[0, dc * 128:(dc + 1) * 128, 0:512], dw)
    ph.wait("vector", ds)
    ph.op("vector", lambda e: e.memset(epsc[:], 1e-5), inc=ve)
    ph.close()

    w13rd = {}
    w13rd[(0, 0, 0)] = dw.v

    def prefetch_w13(ph, tg, e_i, fs, par):
        for dc in range(8):
            ph.dma("gpsimd", w13b[:, par * 8192 + dc * 512: par * 8192 + (dc + 1) * 512],
                   w1e[e_i, dc * 128:(dc + 1) * 128, fs * 512:(fs + 1) * 512], dw)
            ph.dma("gpsimd", w13b[:, par * 8192 + 4096 + dc * 512: par * 8192 + 4096 + (dc + 1) * 512],
                   w3e[e_i, dc * 128:(dc + 1) * 128, fs * 512:(fs + 1) * 512], dw)
        w13rd[(tg, e_i, fs)] = dw.v

    for tg in range(4):
        ph = Ph(ctx)
        xw = ds.v
        for t8 in range(8):
            xw = ph.dma("sync", x3tg[:, t8 * D:(t8 + 1) * D],
                        x3[(tg * 8 + t8) * 128:(tg * 8 + t8 + 1) * 128, :], ds)
        ph.wait("tensor", ds, xw)
        if tg == 0:
            ph.wait("tensor", dw, dw.v)
        for t8 in range(8):
            for half in range(2):
                for f8 in range(4):
                    ph.op("tensor", lambda e, t8=t8, half=half, f8=f8: e.transpose(
                        PTb[:, half * 1024 + f8 * 128: half * 1024 + (f8 + 1) * 128],
                        x3tg[:, t8 * D + half * 512 + f8 * 128: t8 * D + half * 512 + (f8 + 1) * 128],
                        idb[:]), inc=pe if f8 == 3 else None)
                ph.wait("vector", pe)
                for f8 in range(4):
                    fidx = half * 4 + f8
                    ph.op("vector", lambda e, t8=t8, fidx=fidx, half=half, f8=f8: e.tensor_copy(
                        ufm[:, fidx * TG + t8 * 128: fidx * TG + (t8 + 1) * 128],
                        PTb[:, half * 1024 + f8 * 128: half * 1024 + (f8 + 1) * 128]),
                        inc=ve if f8 == 3 else None)
                ph.wait("tensor", ve)
        for t8 in range(8):
            ph.wait("tensor", ve, ve.v)
            for dc in range(8):
                ph.op("tensor", lambda e, t8=t8, dc=dc: e.matmul(
                    PG[:, 0:8], ufm[:, dc * TG + t8 * 128: dc * TG + (t8 + 1) * 128],
                    gwb[:, dc * 8:(dc + 1) * 8], start=(dc == 0), stop=(dc == 7)),
                    inc=pe if dc == 7 else None)
            ph.wait("vector", pe)
            ph.wait("scalar", pe)
            softmax_emit(ph, prb[:], PG[:, 0:8], lncol, pe, ve, ac)
            ph.wait("vector", ve)
            top2_emit(ph, ctx, prb[:], gts[:, t8 * 9:t8 * 9 + 8], lncol[:, 4:8],
                      pe, ve, ac, 0.5)
            ph.op("vector", lambda e, t8=t8: e.memset(
                gts[:, t8 * 9 + 8:t8 * 9 + 9], 1.0 / 16), inc=ve)
        ph.op("vector", lambda e: e.memset(ytg[:], 0.0), inc=ve)
        ph.close()

        for e_i in range(9):
            for h2 in range(2):
                ph = Ph(ctx)
                for fc in range(16):
                    ph.dma("gpsimd", w2b[:, fc * D:(fc + 1) * D],
                           w2e[e_i, h2 * 2048 + fc * 128: h2 * 2048 + (fc + 1) * 128, :], dw)
                w2rd = dw.v
                ev_h = {}
                for fs4 in range(4):
                    fs = h2 * 4 + fs4
                    par = fs % 2
                    nfs = fs + 1
                    ne, ntg2 = e_i, tg
                    if nfs == 8:
                        nfs = 0
                        ne = e_i + 1
                        if ne == 9:
                            ne = 0
                            ntg2 = tg + 1
                    if ntg2 < 4:
                        ph.wait("gpsimd", pe, pe.v)
                        prefetch_w13(ph, ntg2, ne, nfs, 1 - par)
                    ph.wait("tensor", dw, w13rd[(tg, e_i, fs)])
                    for ft4 in range(4):
                        for tns in range(2):
                            it = fs4 * 8 + ft4 * 2 + tns
                            pa = PF[:, (it % 2) * 512:(it % 2) * 512 + 512]
                            pb = PF[:, 1024 + (it % 2) * 512:1024 + (it % 2) * 512 + 512]
                            if it - 2 in ev_h:
                                ph.wait("tensor", ve, ev_h[it - 2])
                            for dc in range(8):
                                ph.op("tensor", lambda e, dc=dc, ft4=ft4, tns=tns, par=par, pa=pa: e.matmul(
                                    pa, w13b[:, par * 8192 + dc * 512 + ft4 * 128:
                                             par * 8192 + dc * 512 + ft4 * 128 + 128],
                                    ufm[:, dc * TG + tns * 512: dc * TG + (tns + 1) * 512],
                                    start=(dc == 0), stop=(dc == 7)))
                            for dc in range(8):
                                ph.op("tensor", lambda e, dc=dc, ft4=ft4, tns=tns, par=par, pb=pb: e.matmul(
                                    pb, w13b[:, par * 8192 + 4096 + dc * 512 + ft4 * 128:
                                             par * 8192 + 4096 + dc * 512 + ft4 * 128 + 128],
                                    ufm[:, dc * TG + tns * 512: dc * TG + (tns + 1) * 512],
                                    start=(dc == 0), stop=(dc == 7)),
                                    inc=pe if dc == 7 else None)
                            ph.wait("scalar", pe)
                            ph.op("scalar", lambda e, pa=pa: e.activation(
                                scr[:], pa, AF.Silu), inc=ac)
                            ph.wait("vector", ac)
                            fidx = fs4 * 4 + ft4
                            ph.op("vector", lambda e, fidx=fidx, tns=tns, pb=pb: e.tensor_tensor(
                                hbuf[:, fidx * TG + tns * 512: fidx * TG + (tns + 1) * 512],
                                scr[:], pb, ALU.mult), inc=ve)
                            ev_h[it] = ve.v
                ph.close()

                ph = Ph(ctx)
                ph.wait("tensor", dw, w2rd)
                ev_w2 = {}
                for t8 in range(8):
                    bank = (t8 % 2) * 1024
                    if t8 >= 2:
                        ph.wait("tensor", ve, ev_w2[t8 - 2])
                    for fc in range(16):
                        for ns in range(2):
                            ph.op("tensor", lambda e, t8=t8, fc=fc, ns=ns, bank=bank: e.matmul(
                                PF[:, bank + ns * 512: bank + (ns + 1) * 512],
                                hbuf[:, fc * TG + t8 * 128: fc * TG + (t8 + 1) * 128],
                                w2b[:, fc * D + ns * 512: fc * D + (ns + 1) * 512],
                                start=(fc == 0), stop=(fc == 15)),
                                inc=pe if (fc == 15 and ns == 1) else None)
                    ph.wait("vector", pe)
                    gcol = gts[:, t8 * 9 + e_i: t8 * 9 + e_i + 1]
                    ph.op("vector", lambda e, t8=t8, bank=bank, gcol=gcol: e.scalar_tensor_tensor(
                        ytg[:, t8 * D:(t8 + 1) * D], PF[:, bank:bank + 1024], gcol,
                        ytg[:, t8 * D:(t8 + 1) * D], ALU.mult, ALU.add), inc=ve)
                    ev_w2[t8] = ve.v
                ph.close()

        ph = Ph(ctx)
        for t8 in range(8):
            tt = tg * 8 + t8
            ph.wait("vector", ac)
            ph.op("vector", lambda e, t8=t8: e.tensor_add(
                sres[:], x3tg[:, t8 * D:(t8 + 1) * D], ytg[:, t8 * D:(t8 + 1) * D]), inc=ve)
            ph.wait("vector", st, st.v)
            ln_emit(ph, sres[:], outb[:, (t8 % 2) * D:(t8 % 2 + 1) * D], lncol, scre[:],
                    pe, ve, ac, epsc[:, 0:1])
            ph.wait("sync", ve)
            ph.dma("sync", out[tt * 128:(tt + 1) * 128, :], outb[:, (t8 % 2) * D:(t8 % 2 + 1) * D], st)
        ph.close()

    ph = Ph(ctx)
    ph.wait("sync", st)
    ph.close()
    return nc, ctx


def prep_l1ffn(inputs, x3_parts):
    w1s = np.asarray(inputs["L1_ffn_w1_s"], np.float32)
    w3s = np.asarray(inputs["L1_ffn_w3_s"], np.float32)
    w2s = np.asarray(inputs["L1_ffn_w2_s"], np.float32)
    w1cat = np.concatenate([w1s[n] for n in range(8)], axis=1)[None]
    w3cat = np.concatenate([w3s[n] for n in range(8)], axis=1)[None]
    w2cat = np.concatenate([w2s[n] for n in range(8)], axis=0)[None]
    shared = dict(
        gwT=np.ascontiguousarray(np.asarray(inputs["L1_ffn_gate_W"], np.float32).T),
        w1e=np.ascontiguousarray(np.concatenate([np.asarray(inputs["L1_ffn_w1_e"], np.float32), w1cat], 0)),
        w3e=np.ascontiguousarray(np.concatenate([np.asarray(inputs["L1_ffn_w3_e"], np.float32), w3cat], 0)),
        w2e=np.ascontiguousarray(np.concatenate([np.asarray(inputs["L1_ffn_w2_e"], np.float32), w2cat], 0)),
        identb=np.eye(128, dtype=BF),
    )
    if x3_parts is None:
        return [dict(shared) for c in range(8)]
    return [dict(shared, x3=x3_parts[c]) for c in range(8)]


# version 06 (final)
# speedup vs baseline: 1.0332x; 1.0332x over previous
import numpy as np
import ml_dtypes
from contextlib import ExitStack
import concourse.bass as bass
import concourse.mybir as mybir
from concourse.bass_utils import run_bass_kernel_spmd

F32 = mybir.dt.float32
BF16 = mybir.dt.bfloat16
AF = mybir.ActivationFunctionType
ALU = mybir.AluOpType
AX = mybir.AxisListType
BF = ml_dtypes.bfloat16

T = 4096
NTT = 32
D = 1024
CH = 256
NCH = 16
DIN = 2048
ENGS = ("sync", "gpsimd", "tensor", "scalar", "vector")
RG = [[0, 1, 2, 3], [4, 5, 6, 7]]


class Sem:
    def __init__(self, nc, es, name):
        self.h = es.enter_context(nc.semaphore(name))
        self.v = 0


class Ctx:
    def __init__(self, nc):
        self.nc = nc
        self.es = ExitStack()
        self.scopes = [self.es]
        self._sems = {}
        self._ps = {}
        self.t = {}
        self.uid = 0

    def sem(self, name):
        if name not in self._sems:
            self._sems[name] = Sem(self.nc, self.es, name)
        return self._sems[name]

    def push_scope(self):
        s = ExitStack()
        self.scopes.append(s)
        return s

    def pop_scope(self):
        self.scopes.pop().close()

    def sb(self, name, shape, dtype, scope=None):
        self.uid += 1
        h = (scope or self.scopes[-1]).enter_context(
            self.nc.sbuf_tensor(f"{name}_{self.uid}", shape, dtype))
        self.t[name] = h
        return h

    def ps(self, name, shape, dtype, scope=None):
        if name not in self._ps:
            self._ps[name] = self.es.enter_context(self.nc.psum_tensor(name, shape, dtype))
        return self._ps[name]


class Ph:

    def __init__(self, ctx):
        self.ctx = ctx
        self.q = {e: [] for e in ENGS}

    def wait(self, eng, sem, v=None):
        vv = sem.v if v is None else v
        self.q[eng].append(lambda e, h=sem.h, vv=vv: e.wait_ge(h, vv))

    def op(self, eng, fn, inc=None, amt=1):
        if inc is not None:
            inc.v += amt
            self.q[eng].append(
                lambda e, fn=fn, h=inc.h, amt=amt: fn(e).then_inc(h, amt))
        else:
            self.q[eng].append(lambda e, fn=fn: fn(e))

    def dma(self, eng, out, in_, sem, **kw):
        sem.v += 16
        self.q[eng].append(
            lambda e, out=out, in_=in_, h=sem.h, kw=kw:
                e.dma_start(out=out, in_=in_, **kw).then_inc(h, 16))
        return sem.v

    def close(self):
        with self.ctx.nc.Block() as blk:
            for e in ENGS:
                if self.q[e]:
                    def runner(eng, q=tuple(self.q[e])):
                        for fn in q:
                            fn(eng)
                    getattr(blk, e)(runner)


def ln_emit(ph, sres, outap, cols, scr, pe, ve, ac, epsap=None, snap=None):
    ph.op("vector", lambda e: e.reduce_sum(cols[:, 0:1], sres, axis=AX.X), inc=ve)
    ph.wait("scalar", ve)
    ph.op("scalar", lambda e: e.activation(scr, sres, AF.Square,
                                           accum_out=cols[:, 1:2]), inc=ac)
    ph.wait("vector", ac)
    ph.op("vector", lambda e: e.tensor_scalar_mul(cols[:, 2:3], cols[:, 0:1], 1.0 / 1024))
    ph.op("vector", lambda e: e.tensor_scalar_mul(cols[:, 3:4], cols[:, 1:2], 1.0 / 1024), inc=ve)
    ph.wait("vector", ve)
    ph.op("vector", lambda e: e.tensor_mul(cols[:, 4:5], cols[:, 2:3], cols[:, 2:3]), inc=ve)
    ph.wait("vector", ve)
    ph.op("vector", lambda e: e.tensor_sub(cols[:, 5:6], cols[:, 3:4], cols[:, 4:5]), inc=ve)
    if snap is not None:
        ph.op("vector", lambda e: e.tensor_copy(snap, cols[:, 0:8]), inc=ve)
    ph.wait("scalar", ve)
    ph.op("scalar", lambda e: e.activation(cols[:, 6:7], cols[:, 5:6], AF.Sqrt,
                                           bias=epsap), inc=ac)
    ph.wait("vector", ac)
    ph.op("vector", lambda e: e.reciprocal(cols[:, 7:8], cols[:, 6:7]), inc=ve)
    ph.wait("vector", ve)
    ph.op("vector", lambda e: e.tensor_scalar(
        outap, sres, cols[:, 2:3], cols[:, 7:8], ALU.subtract, ALU.mult), inc=ve)


def _env(env):
    if env is not None:
        return env
    nc = bass.Bass("TRN2", target_bir_lowering=False,
                   detect_race_conditions=False)
    ctx = Ctx(nc)

    def dram(n, s, d, k):
        return nc.dram_tensor(n, s, d,
                              kind="ExternalInput" if k == "in" else "ExternalOutput").ap()
    return nc, ctx, dram



def build_l0attn(dbg=False, env=None):
    nc, ctx, dram = _env(env)
    ei = lambda n, s, d: dram(n, s, d, "in")
    eo = lambda n, s, d: dram(n, s, d, "out")

    xinT = ei("xinT", [D, 4 + T], F32)
    xin = ei("xin", [T, D], F32)
    w_in = ei("w_in", [D, 4384], F32)
    cdiag_h = ei("cdiag", [72, 128, 128], F32)
    convb = ei("convb", [128, 18], F32)
    dtb = ei("dtb", [32, 1], F32)
    alog = ei("alog", [32, 1], F32)
    drep = ei("drep", [128, DIN], F32)
    nrmrep = ei("nrmrep", [128, DIN], F32)
    w_out = ei("w_out", [DIN, D], F32)
    identb = ei("identb", [128, 128], BF16)
    identf = ei("identf", [128, 128], F32)
    maskT = ei("maskT", [128, 256], BF16)
    cmask = ei("cmask", [32, 3], F32)
    m2 = ei("m2", [32, 12], F32)
    onesf = ei("onesf", [1, 128], F32)

    x1 = eo("x1", [T, D], BF16)

    yhbm = nc.dram_tensor("yhbm", [T, DIN], BF16,
                          kind="ExternalOutput" if dbg else "Internal").ap()
    agsrc = nc.dram_tensor("agsrc", [128, 2052], F32).ap()
    agdst = nc.dram_tensor("agdst", [512, 2052], F32).ap()

    cvb = ctx.sb("cvb", [128, 18], F32)
    dtbc = ctx.sb("dtbc", [32, 1], F32)
    acolb = ctx.sb("acolb", [32, 1], F32)
    dbc = ctx.sb("dbc", [128, DIN], BF16)
    idb = ctx.sb("idb", [128, 128], BF16)
    idf = ctx.sb("idf", [128, 128], F32)
    mskb = ctx.sb("mskb", [128, 256], BF16)
    onf = ctx.sb("onf", [1, 128], F32)
    cmb = ctx.sb("cmb", [32, 3], F32)
    m2b = ctx.sb("m2b", [32, 12], F32)
    ybuf = ctx.sb("ybuf", [128, 4 * DIN], BF16)
    csv = ctx.sb("csv", [128, T], BF16)
    datm = ctx.sb("datm", [128, NTT * 96], F32)
    atot = ctx.sb("atot", [32, NCH], F32)
    runacs = ctx.sb("runacs", [32, 1], F32)
    rst = ctx.sb("rst", [128, DIN], F32)
    rstbf = ctx.sb("rstbf", [128, DIN], BF16)
    scry = ctx.sb("scry", [128, 1024], F32)
    zz32 = ctx.sb("zz32", [32, CH], F32)
    xfmb = ctx.sb("xfmb", [128, 2 * 8 * 260], BF16)
    grow = ctx.sb("grow", [1, 32], F32)
    epsc = ctx.sb("epsc", [128, 1], F32)

    PF = ctx.ps("PF", [128, 2048], F32)
    PG = ctx.ps("PG", [128, 1024], F32)
    PTb = ctx.ps("PTb", [128, 2048], BF16)

    dw = ctx.sem("dw")
    dx = ctx.sem("dx")
    ds = ctx.sem("ds")
    pe = ctx.sem("pe")
    ve = ctx.sem("ve")
    ac = ctx.sem("ac")
    cc = ctx.sem("cc")
    st = ctx.sem("st")

    xfm = lambda par, dc: xfmb[:, par * 2080 + dc * 260: par * 2080 + (dc + 1) * 260]

    chs = ExitStack()
    wbc = ctx.sb("wbc", [128, 8 * 2336], BF16, chs)
    wdt = ctx.sb("wdt", [128, 8 * 32], BF16, chs)
    cdg = ctx.sb("cdg", [128, 72 * 128], BF16, chs)
    xbcc = ctx.sb("xbcc", [128, 18 * 260], BF16, chs)
    bfm = ctx.sb("bfm", [128, 256], BF16, chs)
    btm = ctx.sb("btm", [128, 256], BF16, chs)
    xstm = ctx.sb("xstm", [128, 2 * DIN], BF16, chs)
    xsd = ctx.sb("xsd", [128, DIN], BF16, chs)
    dtc = ctx.sb("dtc", [32, 260], F32, chs)
    dta = ctx.sb("dta", [32, 260], F32, chs)
    acsc = ctx.sb("acsc", [32, CH], F32, chs)
    asc = ctx.sb("asc", [32, CH], F32, chs)
    aec = ctx.sb("aec", [32, CH], F32, chs)
    cbmt = ctx.sb("cbmt", [128, 384], BF16, chs)
    mtb = ctx.sb("mtb", [128, 8 * 384], BF16, chs)
    odc = ctx.sb("odc", [128, 64], F32, chs)
    decc = ctx.sb("decc", [128, 64], F32, chs)
    scrc = ctx.sb("scrc", [128, 1024], F32, chs)
    scrybf = scry[:].bitcast(BF16)

    ph = Ph(ctx)
    for dc in range(8):
        ph.dma("gpsimd", wbc[:, dc * 2336:(dc + 1) * 2336],
               w_in[dc * 128:(dc + 1) * 128, 2048:4384], dw)
        ph.dma("gpsimd", wdt[:, dc * 32:(dc + 1) * 32],
               w_in[dc * 128:(dc + 1) * 128, 4352:4384], dw)
    ph.dma("gpsimd", cdg[:].rearrange("p (t f) -> p t f", t=72),
           cdiag_h.rearrange("t p f -> p t f"), dw)
    ph.dma("gpsimd", dbc[:], drep[:], dw)
    for dst, src in ((cvb, convb), (dtbc, dtb), (idb, identb), (idf, identf),
                     (mskb, maskT), (onf, onesf), (cmb, cmask), (m2b, m2)):
        ph.dma("sync", dst[:], src, ds)
    for dc in range(8):
        ph.dma("gpsimd", xfm(0, dc), xinT[dc * 128:(dc + 1) * 128, 0:260], dx)
    ph.dma("sync", acolb[:], alog, ds)
    ph.wait("vector", dx)
    ph.wait("vector", ds)
    ph.wait("vector", dw)
    ph.op("vector", lambda e: e.memset(zz32[:], 0.0))
    ph.op("vector", lambda e: e.memset(runacs[:], 0.0))
    ph.op("vector", lambda e: e.memset(epsc[:], 1e-5))
    ph.op("vector", lambda e: e.memset(rst[:], 0.0))
    ph.op("vector", lambda e: e.memset(rstbf[:], 0.0), inc=ve)
    ph.wait("scalar", ds)
    ph.op("scalar", lambda e: e.activation(acolb[:], acolb[:], AF.Exp), inc=ac)
    ph.wait("vector", ac)
    ph.op("vector", lambda e: e.tensor_scalar_mul(acolb[:], acolb[:], -1.0), inc=ve)
    ph.close()

    ltc = lambda gl: (gl * 96, gl * 96 + 32, gl * 96 + 64)


    ystore = {}
    for c in range(NCH):
        par = c % 2
        ph = Ph(ctx)
        xw = dx.v
        if c + 1 < NCH:
            for dc in range(8):
                ph.dma("gpsimd", xfm(1 - par, dc),
                       xinT[dc * 128:(dc + 1) * 128, (c + 1) * 256:(c + 1) * 256 + 260], dx)
        ph.wait("tensor", dx, xw)
        if c == 0:
            ph.wait("tensor", dw, dw.v)
            ph.wait("tensor", ds, ds.v)
        ev_ip = {}
        ac_ip = {}
        for ot in range(18):
            ipb = PF[:, (ot % 2) * 512:(ot % 2) * 512 + 260]
            cvp = PF[:, 1024 + (ot % 2) * 512:1024 + (ot % 2) * 512 + 256]
            if ot >= 2:
                ph.wait("tensor", ve, ev_ip[ot - 2])
                ph.wait("tensor", ac, ac_ip[ot - 2])
            for dc in range(8):
                ph.op("tensor", lambda e, ot=ot, dc=dc, ipb=ipb, par=par: e.matmul(
                    ipb, wbc[:, dc * 2336 + ot * 128: dc * 2336 + ot * 128 + 128],
                    xfm(par, dc), start=(dc == 0), stop=(dc == 7)),
                    inc=pe if dc == 7 else None)
            ph.wait("vector", pe)
            ph.op("vector", lambda e, ot=ot, ipb=ipb: e.tensor_copy(
                xbcc[:, ot * 260:(ot + 1) * 260], ipb), inc=ve)
            ph.wait("tensor", ve)
            for k in range(4):
                ph.op("tensor", lambda e, ot=ot, k=k, cvp=cvp: e.matmul(
                    cvp, cdg[:, (ot * 4 + k) * 128:(ot * 4 + k) * 128 + 128],
                    xbcc[:, ot * 260 + 1 + k: ot * 260 + 257 + k],
                    start=(k == 0), stop=(k == 3)),
                    inc=pe if k == 3 else None)
            ph.wait("scalar", pe)
            if ot < 16:
                dst = xbcc[:, ot * 260 + 4:(ot + 1) * 260]
            elif ot == 16:
                dst = bfm[:]
            else:
                dst = csv[:, c * 256:(c + 1) * 256]
            ph.op("scalar", lambda e, cvp=cvp, dst=dst, ot=ot: e.activation(
                dst, cvp, AF.Silu, bias=cvb[:, ot:ot + 1]), inc=ac)
            ev_ip[ot] = ve.v
            ac_ip[ot] = ac.v

        dtp = PG[:, 0:260]
        for dc in range(8):
            ph.op("tensor", lambda e, dc=dc, par=par: e.matmul(
                dtp[0:32, :], wdt[:, dc * 32:(dc + 1) * 32], xfm(par, dc),
                start=(dc == 0), stop=(dc == 7)), inc=pe if dc == 7 else None)
        ph.wait("vector", pe)
        ph.op("vector", lambda e: e.tensor_scalar(
            dtc[:], dtp[0:32, :], dtbc[:, 0:1], None, ALU.add), inc=ve)
        ph.wait("scalar", ve)
        ph.op("scalar", lambda e: e.activation(dta[:], dtc[:], AF.Abs))
        ph.op("scalar", lambda e: e.activation(dta[:], dta[:], AF.Exp, scale=-1.0))
        ph.op("scalar", lambda e: e.activation(dta[:], dta[:], AF.Ln, bias=1.0), inc=ac)
        ph.wait("vector", ac)
        ph.op("vector", lambda e: e.tensor_scalar_max(dtc[:], dtc[:], 0.0))
        ph.op("vector", lambda e: e.tensor_add(dtc[:], dtc[:], dta[:]))
        ph.op("vector", lambda e: e.tensor_scalar(
            dta[:], dtc[:], acolb[:, 0:1], None, ALU.mult))
        ph.op("vector", lambda e: e.tensor_tensor_scan(
            acsc[:], dta[:, 4:260], zz32[:], runacs[:, 0:1], ALU.add, ALU.add))
        ph.op("vector", lambda e: e.tensor_scalar(
            asc[:], acsc[:], runacs[:, 0:1], None, ALU.subtract))
        ph.op("vector", lambda e: e.tensor_scalar(
            aec[:], acsc[:], acsc[:, 255:256], None, ALU.subtract))
        ph.op("vector", lambda e, c=c: e.tensor_copy(atot[:, c:c + 1], asc[:, 255:256]))
        ph.op("vector", lambda e: e.tensor_copy(runacs[:], acsc[:, 255:256]), inc=ve)

        ph.wait("tensor", ve)
        ph.wait("tensor", ac)
        for lt in range(2):
            gl = 2 * c + lt
            c0 = ltc(gl)[0]
            sl = PG[:, 512:512 + 96]
            ph.op("tensor", lambda e, lt=lt, sl=sl: e.transpose(
                sl[:, 0:32], dtc[:, 4 + lt * 128:4 + lt * 128 + 128], idf[0:32, 0:32]))
            ph.op("tensor", lambda e, lt=lt, sl=sl: e.transpose(
                sl[:, 32:64], asc[:, lt * 128:lt * 128 + 128], idf[0:32, 0:32]))
            ph.op("tensor", lambda e, lt=lt, sl=sl: e.transpose(
                sl[:, 64:96], aec[:, lt * 128:lt * 128 + 128], idf[0:32, 0:32]), inc=pe)
            ph.wait("vector", pe)
            ph.op("vector", lambda e, sl=sl, c0=c0: e.tensor_copy(
                datm[:, c0:c0 + 96], sl[:]), inc=ve)
            ph.wait("tensor", ve)

        for lt in range(2):
            ph.op("tensor", lambda e, lt=lt: e.transpose(
                PTb[:, 0:128], bfm[:, lt * 128:lt * 128 + 128], idb[:]), inc=pe)
            ph.wait("vector", pe)
            ph.op("vector", lambda e, lt=lt: e.tensor_copy(
                btm[:, lt * 128:lt * 128 + 128], PTb[:, 0:128]), inc=ve)
            ph.wait("tensor", ve)
            for half in range(2):
                base = 1024
                for f8 in range(8):
                    ft = half * 8 + f8
                    ph.op("tensor", lambda e, ft=ft, lt=lt, f8=f8, base=base: e.transpose(
                        PTb[:, base + f8 * 128: base + (f8 + 1) * 128],
                        xbcc[:, ft * 260 + 4 + lt * 128: ft * 260 + 4 + lt * 128 + 128],
                        idb[:]), inc=pe if f8 == 7 else None)
                ph.wait("vector", pe)
                ph.op("vector", lambda e, lt=lt, half=half, base=base: e.tensor_copy(
                    xstm[:, lt * 2048 + half * 1024: lt * 2048 + (half + 1) * 1024],
                    PTb[:, base:base + 1024]), inc=ve)
                ph.wait("tensor", ve)
            gl = 2 * c + lt
            c0, c1, c2 = ltc(gl)
            ph.wait("scalar", ve)
            ph.op("scalar", lambda e, c1=c1, lt=lt: e.activation(
                odc[:, lt * 32:(lt + 1) * 32], datm[:, c1:c1 + 32], AF.Exp))
            ph.op("scalar", lambda e, c2=c2, lt=lt: e.activation(
                decc[:, lt * 32:(lt + 1) * 32], datm[:, c2:c2 + 32], AF.Exp, scale=-1.0),
                inc=ac)
            xv = xstm[:, lt * 2048:(lt + 1) * 2048]
            yv = ybuf[:, (par * 2 + lt) * 2048:(par * 2 + lt + 1) * 2048]
            ph.wait("vector", ac)
            ph.wait("vector", st, ystore.get(c - 2, 0))
            ph.op("vector", lambda e, xv=xv, yv=yv: e.tensor_tensor(yv, xv, dbc[:], ALU.mult))
            dtv = datm[:, c0:c0 + 32].unsqueeze(2).broadcast_to([128, 32, 64])
            xvh = xv.rearrange("p (h q) -> p h q", h=32)
            ph.op("vector", lambda e, xvh=xvh, dtv=dtv: e.tensor_tensor(
                xvh, xvh, dtv, ALU.mult), inc=ve)
        ph.close()

        ph = Ph(ctx)
        ph.op("tensor", lambda e, c=c: e.transpose(
            PG[0:1, 0:32], atot[:, c:c + 1], idf[0:32, 0:32]), inc=pe)
        ph.wait("scalar", pe)
        ph.op("scalar", lambda e: e.activation(grow[:], PG[0:1, 0:32], AF.Exp), inc=ac)
        for stile in range(2):
            dstp = PF[:, 1024 + stile * 512: 1024 + stile * 512 + (256 if stile == 0 else 128)]
            ph.op("tensor", lambda e, stile=stile, dstp=dstp, c=c: e.matmul(
                dstp, bfm[:, stile * 128:stile * 128 + 128],
                csv[:, c * 256 + stile * 128:(c + 1) * 256],
                start=True, stop=True), inc=pe)
        ph.wait("vector", pe)
        ph.op("vector", lambda e: e.tensor_tensor(
            cbmt[:, 0:256], PF[:, 1024:1280], mskb[:], ALU.mult))
        ph.op("vector", lambda e: e.tensor_tensor(
            cbmt[:, 256:384], PF[:, 1536:1664], mskb[:, 0:128], ALU.mult), inc=ve)

        for lt in range(2):
            ph.wait("tensor", ve, ve.v)
            for ns in range(4):
                ph.op("tensor", lambda e, lt=lt, ns=ns, c=c: e.matmul(
                    PF[:, ns * 512:(ns + 1) * 512],
                    csv[:, c * 256 + lt * 128: c * 256 + lt * 128 + 128],
                    rstbf[:, ns * 512:(ns + 1) * 512],
                    start=True, stop=True), inc=pe if ns == 3 else None)
            ph.wait("vector", pe)
            odv = odc[:, lt * 32:(lt + 1) * 32].unsqueeze(2).broadcast_to([128, 32, 64])
            yv = ybuf[:, (par * 2 + lt) * 2048:(par * 2 + lt + 1) * 2048]
            ph.op("vector", lambda e, odv=odv: e.tensor_tensor(
                scrybf.rearrange("p (h q) -> p h q", h=32),
                PF[:, 0:2048].rearrange("p (h q) -> p h q", h=32), odv, ALU.mult))
            ph.op("vector", lambda e, yv=yv: e.tensor_add(yv, yv, scrybf), inc=ve)

        ph.wait("tensor", ac)
        tsval = {}
        for hg in range(4):
            for h8 in range(8):
                h = hg * 8 + h8
                gp = h8 % 2
                r2 = PG[:, 0:256] if gp == 0 else PG[:, 512:768]
                sc = scrc[:, gp * 512: gp * 512 + 384]
                gl0, gl1 = 2 * c, 2 * c + 1
                a0c = ltc(gl0)[1]
                a1c = ltc(gl1)[1]
                if h - 2 in tsval:
                    ph.wait("tensor", ve, tsval[h - 2])
                elif h == 0:
                    ph.wait("tensor", ve, ve.v)
                ph.op("tensor", lambda e, h=h, r2=r2, a0c=a0c: e.transpose(
                    r2[:, 0:128], datm[:, a0c + h:a0c + h + 1].broadcast_to([128, 128]),
                    idf[:]))
                ph.op("tensor", lambda e, h=h, r2=r2, a1c=a1c: e.transpose(
                    r2[:, 128:256], datm[:, a1c + h:a1c + h + 1].broadcast_to([128, 128]),
                    idf[:]), inc=pe)
                ph.wait("vector", pe)
                ph.op("vector", lambda e, r2=r2, a0c=a0c, h=h, sc=sc: e.tensor_scalar(
                    sc[:, 0:256], r2, datm[:, a0c + h:a0c + h + 1], 0.0,
                    ALU.subtract, ALU.min))
                ph.op("vector", lambda e, r2=r2, a1c=a1c, h=h, sc=sc: e.tensor_scalar(
                    sc[:, 256:384], r2[:, 128:256], datm[:, a1c + h:a1c + h + 1], 0.0,
                    ALU.subtract, ALU.min), inc=ve)
                tsval[h] = ve.v
                ph.wait("scalar", ve)
                ph.op("scalar", lambda e, sc=sc: e.activation(sc[:, 0:256], sc[:, 0:256], AF.Exp))
                ph.op("scalar", lambda e, sc=sc: e.activation(sc[:, 256:384], sc[:, 256:384], AF.Exp), inc=ac)
                ph.wait("vector", ac)
                ph.op("vector", lambda e, h8=h8, sc=sc: e.tensor_tensor(
                    mtb[:, h8 * 384:h8 * 384 + 256], sc[:, 0:256], cbmt[:, 0:256], ALU.mult))
                ph.op("vector", lambda e, h8=h8, sc=sc: e.tensor_tensor(
                    mtb[:, h8 * 384 + 256:h8 * 384 + 384], sc[:, 256:384], cbmt[:, 256:384],
                    ALU.mult), inc=ve)
            ph.wait("tensor", ve)
            for lt in range(2):
                bank = PF[:, (lt * 2) * 512:(lt * 2) * 512 + 512]
                for h8 in range(8):
                    h = hg * 8 + h8
                    if lt == 0:
                        ph.op("tensor", lambda e, h8=h8, bank=bank, h=h: e.matmul(
                            bank[:, h8 * 64:(h8 + 1) * 64],
                            mtb[:, h8 * 384:h8 * 384 + 128],
                            xstm[:, h * 64:h * 64 + 64],
                            start=True, stop=True),
                            inc=pe if h8 == 7 else None)
                    else:
                        ph.op("tensor", lambda e, h8=h8, bank=bank, h=h: e.matmul(
                            bank[:, h8 * 64:(h8 + 1) * 64],
                            mtb[:, h8 * 384 + 128:h8 * 384 + 256],
                            xstm[:, h * 64:h * 64 + 64],
                            start=True, stop=False))
                        ph.op("tensor", lambda e, h8=h8, bank=bank, h=h: e.matmul(
                            bank[:, h8 * 64:(h8 + 1) * 64],
                            mtb[:, h8 * 384 + 256:h8 * 384 + 384],
                            xstm[:, 2048 + h * 64:2048 + h * 64 + 64],
                            start=False, stop=True),
                            inc=pe if h8 == 7 else None)
                ph.wait("vector", pe)
                yv = ybuf[:, (par * 2 + lt) * 2048 + hg * 512:(par * 2 + lt) * 2048 + (hg + 1) * 512]
                ph.op("vector", lambda e, yv=yv, bank=bank: e.tensor_add(yv, yv, bank), inc=ve)

        for half in range(2):
            hpc = slice(half * 1024, (half + 1) * 1024)
            sp = PG[:, 0:1024]
            for lt in range(2):
                dv = (decc[:, lt * 32 + half * 16: lt * 32 + half * 16 + 16]
                      .unsqueeze(2).broadcast_to([128, 16, 64]))
                ph.op("vector", lambda e, lt=lt, half=half, dv=dv: e.tensor_tensor(
                    xsd[:, lt * 1024:(lt + 1) * 1024].rearrange("p (h q) -> p h q", h=16),
                    xstm[:, lt * 2048 + half * 1024: lt * 2048 + (half + 1) * 1024]
                        .rearrange("p (h q) -> p h q", h=16),
                    dv, ALU.mult), inc=ve)
            ph.wait("tensor", ve)
            for ns in range(2):
                for lt in range(2):
                    ph.op("tensor", lambda e, ns=ns, lt=lt, sp=sp: e.matmul(
                        sp[:, ns * 512:(ns + 1) * 512],
                        btm[:, lt * 128:lt * 128 + 128],
                        xsd[:, lt * 1024 + ns * 512: lt * 1024 + (ns + 1) * 512],
                        start=(lt == 0), stop=(lt == 1)),
                        inc=pe if (lt == 1) else None)
            gvv = (grow[:, half * 16:(half + 1) * 16]
                   .unsqueeze(2).broadcast_to([1, 16, 64]))
            for ns in range(2):
                ph.op("tensor", lambda e, gvv=gvv, ns=ns: e.matmul(
                    PF[:, 1024 + ns * 512:1024 + (ns + 1) * 512],
                    onf[0:1, :],
                    gvv[:, ns * 8:(ns + 1) * 8, :],
                    start=True, stop=True), inc=pe if ns == 1 else None)
            ph.wait("vector", pe)
            ph.op("vector", lambda e, hpc=hpc: e.tensor_tensor(
                rst[:, hpc], rst[:, hpc], PF[:, 1024:2048], ALU.mult))
            ph.op("vector", lambda e, hpc=hpc, sp=sp: e.tensor_add(
                rst[:, hpc], rst[:, hpc], sp[:]))
            ph.op("vector", lambda e, hpc=hpc: e.tensor_copy(
                rstbf[:, hpc], rst[:, hpc]), inc=ve)
        ph.wait("sync", ve)
        for lt in range(2):
            ph.dma("sync", yhbm[(2 * c + lt) * 128:(2 * c + lt + 1) * 128, :],
                   ybuf[:, (par * 2 + lt) * 2048:(par * 2 + lt + 1) * 2048], st)
        ystore[c] = st.v
        ph.close()

        if dbg and c == 0:
            ph = Ph(ctx)
            dbg_list = [
                ("d_xbcc", xbcc, BF16), ("d_bfm", bfm, BF16), ("d_btm", btm, BF16),
                ("d_dtc", dtc, F32), ("d_acsc", acsc, F32), ("d_asc", asc, F32),
                ("d_aec", aec, F32), ("d_xstm", xstm, BF16), ("d_xsd", xsd, BF16),
                ("d_cbmt", cbmt, BF16), ("d_mtb", mtb, BF16), ("d_odc", odc, F32),
                ("d_decc", decc, F32), ("d_rst", rst, F32),
            ]
            ph.wait("gpsimd", ve)
            ph.wait("gpsimd", pe)
            ph.wait("gpsimd", ac)
            for nm, tt_, dt_ in dbg_list:
                shp = list(tt_[:].shape)
                od = nc.dram_tensor(nm, shp, F32, kind="ExternalOutput").ap()
                ph.dma("gpsimd", od, tt_[:], dx)
            odym = nc.dram_tensor("d_ybuf", [128, 2 * 2048], F32, kind="ExternalOutput").ap()
            ph.dma("gpsimd", odym, ybuf[:, 0:2 * 2048], dx)
            oddatm = nc.dram_tensor("d_datm", [128, 192], F32, kind="ExternalOutput").ap()
            ph.dma("gpsimd", oddatm, datm[:, 0:192], dx)
            ph.wait("gpsimd", dx)
            ph.close()

    chs.close()

    wz = ctx.sb("wz", [128, 8 * 2048], BF16)
    wob = ctx.sb("wob", [128, 16 * 1024], BF16)
    nrmb = ctx.sb("nrmb", [128, DIN], BF16)
    ph = Ph(ctx)
    for dc in range(8):
        ph.dma("gpsimd", wz[:, dc * 2048:(dc + 1) * 2048],
               w_in[dc * 128:(dc + 1) * 128, 0:2048], dw)
    for ic in range(16):
        ph.dma("gpsimd", wob[:, ic * 1024:(ic + 1) * 1024],
               w_out[ic * 128:(ic + 1) * 128, :], dw)
    ph.dma("gpsimd", nrmb[:], nrmrep[:], dw)
    ph.dma("sync", agsrc[:, 0:2048], rst[:], st)
    ph.dma("sync", agsrc[0:32, 2048:2049], runacs[:], st, allow_slow_non_contiguous=True)
    ph.wait("gpsimd", st)
    ph.op("gpsimd", lambda e: e.collective_compute(
        "AllGather", ALU.bypass, replica_groups=RG,
        ins=[agsrc.opt()], outs=[agdst.opt()]), inc=cc)
    ph.close()

    fxs = ExitStack()
    fld = ctx.sb("fld", [128, 2048], F32, fxs)
    agc = ctx.sb("agc", [32, 4], F32, fxs)
    rin = ctx.sb("rin", [128, 2048], F32, fxs)
    rinbf = ctx.sb("rinbf", [128, 2048], BF16, fxs)
    fcol = ctx.sb("fcol", [32, 4], F32, fxs)
    scrf = ctx.sb("scrf", [128, 1024], F32, fxs)
    ph = Ph(ctx)
    ph.wait("sync", cc, cc.v)
    for i in range(4):
        ph.dma("sync", agc[:, i:i + 1], agdst[i * 128:i * 128 + 32, 2048:2049], ds,
               allow_slow_non_contiguous=True)
    ph.wait("vector", ds)
    ph.op("vector", lambda e: e.memset(rin[:], 0.0), inc=ve)
    for i in range(3):
        if i > 0:
            ph.wait("sync", ve, ve.v)
        ph.dma("sync", fld[:], agdst[i * 128:(i + 1) * 128, 0:2048], ds)
        ph.wait("vector", ds)
        ph.op("vector", lambda e, i=i: e.tensor_scalar(
            fcol[:, 0:1], agc[:, 0:1], m2b[:, i * 4:i * 4 + 1], None, ALU.mult))
        for j in range(1, 4):
            ph.op("vector", lambda e, i=i, j=j: e.scalar_tensor_tensor(
                fcol[:, 0:1], agc[:, j:j + 1], m2b[:, i * 4 + j:i * 4 + j + 1],
                fcol[:, 0:1], ALU.mult, ALU.add))
        ph.op("vector", lambda e: e.tensor_copy(fcol[:, 1:2], fcol[:, 0:1]), inc=ve)
        ph.wait("scalar", ve)
        ph.op("scalar", lambda e: e.activation(fcol[:, 2:3], fcol[:, 1:2], AF.Exp), inc=ac)
        ph.wait("vector", ac)
        ph.op("vector", lambda e, i=i: e.tensor_scalar(
            fcol[:, 3:4], fcol[:, 2:3], cmb[:, i:i + 1], None, ALU.mult), inc=ve)
        ph.wait("tensor", ve)
        ph.op("tensor", lambda e: e.transpose(PG[0:1, 0:32], fcol[:, 3:4], idf[0:32, 0:32]), inc=pe)
        ph.wait("vector", pe)
        ph.op("vector", lambda e: e.tensor_copy(grow[:], PG[0:1, 0:32]), inc=ve)
        ph.wait("tensor", ve)
        gvv = grow[:].unsqueeze(2).broadcast_to([1, 32, 64])
        for ns in range(4):
            ph.op("tensor", lambda e, ns=ns, gvv=gvv: e.matmul(
                PF[:, ns * 512:(ns + 1) * 512], onf[0:1, :],
                gvv[:, ns * 8:(ns + 1) * 8, :], start=True, stop=True),
                inc=pe if ns == 3 else None)
        ph.wait("vector", pe)
        for hf in range(2):
            ph.op("vector", lambda e, hf=hf: e.tensor_tensor(
                scrf[:], fld[:, hf * 1024:(hf + 1) * 1024],
                PF[:, hf * 1024:(hf + 1) * 1024], ALU.mult))
            ph.op("vector", lambda e, hf=hf: e.tensor_add(
                rin[:, hf * 1024:(hf + 1) * 1024], rin[:, hf * 1024:(hf + 1) * 1024],
                scrf[:]), inc=ve)
    ph.op("vector", lambda e: e.tensor_copy(rinbf[:], rin[:]), inc=ve)
    if dbg:
        od_ = nc.dram_tensor("d_rin", [128, 2048], F32, kind="ExternalOutput").ap()
        ph.wait("gpsimd", ve)
        ph.dma("gpsimd", od_, rin[:], dx)
        odag = nc.dram_tensor("d_agdst", [512, 2052], F32, kind="ExternalOutput").ap()
        ph.dma("gpsimd", odag, agdst, dx)
    ph.close()

    odf = ctx.sb("odf", [128, 64], F32, fxs)

    eps = ExitStack()
    zsil = ctx.sb("zsil", [128, DIN], BF16, eps)
    ubuf = ctx.sb("ubuf", [128, DIN], BF16, eps)
    unfm = ctx.sb("unfm", [128, DIN], BF16, eps)
    xres = ctx.sb("xres", [128, 2 * D], BF16, eps)
    sres = ctx.sb("sres", [128, D], F32, eps)
    scre = ctx.sb("scre", [128, DIN], F32, eps)
    lncol = ctx.sb("lncol", [128, 8], F32, eps)
    x1b = ctx.sb("x1b", [128, 2 * D], BF16, eps)
    ych = ctx.sb("ych", [128, 2 * DIN], BF16, eps)
    if dbg == "epi":
        dbgu = ctx.sb("dbgu", [128, DIN], BF16, eps)
        dbgc = ctx.sb("dbgc", [128, 8], F32, eps)

    xres_rdy = {}
    ph = Ph(ctx)
    for dc in range(8):
        ph.dma("gpsimd", xfm(0, dc), xinT[dc * 128:(dc + 1) * 128, 0:260], dx)
    for tt0 in range(2):
        xres_rdy[tt0] = ph.dma("gpsimd", xres[:, (tt0 % 2) * D:(tt0 % 2 + 1) * D],
                               xin[tt0 * 128:(tt0 + 1) * 128, :], dx)
    ph.close()

    for c in range(NCH):
        par = c % 2
        ph = Ph(ctx)
        xw = dx.v
        if c + 1 < NCH:
            for dc in range(8):
                ph.dma("gpsimd", xfm(1 - par, dc),
                       xinT[dc * 128:(dc + 1) * 128, (c + 1) * 256:(c + 1) * 256 + 260], dx)
        ph.wait("tensor", dx, xw)
        ph.wait("tensor", dw, dw.v)
        ph.wait("vector", dw, dw.v)
        ph.wait("sync", ve, ve.v)
        for lt in range(2):
            yw = ph.dma("sync", ych[:, lt * 2048:(lt + 1) * 2048],
                        yhbm[(2 * c + lt) * 128:(2 * c + lt + 1) * 128, :], ds)
        ph.wait("vector", ds, yw)
        for lt in range(2):
            gl = 2 * c + lt
            c1 = ltc(gl)[1]
            ph.op("scalar", lambda e, c1=c1, lt=lt: e.activation(
                odf[:, lt * 32:(lt + 1) * 32], datm[:, c1:c1 + 32], AF.Exp), inc=ac)
        for lt in range(2):
            ph.wait("tensor", ve, ve.v)
            for ns in range(4):
                ph.op("tensor", lambda e, lt=lt, ns=ns, c=c: e.matmul(
                    PF[:, ns * 512:(ns + 1) * 512],
                    csv[:, c * 256 + lt * 128: c * 256 + lt * 128 + 128],
                    rinbf[:, ns * 512:(ns + 1) * 512],
                    start=True, stop=True), inc=pe if ns == 3 else None)
            ph.wait("vector", pe)
            ph.wait("vector", ac)
            odv = odf[:, lt * 32:(lt + 1) * 32].unsqueeze(2).broadcast_to([128, 32, 64])
            yv = ych[:, lt * 2048:(lt + 1) * 2048]
            ph.op("vector", lambda e, odv=odv: e.tensor_tensor(
                scrybf.rearrange("p (h q) -> p h q", h=32),
                PF[:, 0:2048].rearrange("p (h q) -> p h q", h=32), odv, ALU.mult))
            ph.op("vector", lambda e, yv=yv: e.tensor_add(yv, yv, scrybf), inc=ve)
        ph.wait("tensor", ve)
        ph.op("tensor", lambda e, c=c: e.transpose(PG[0:1, 0:32], atot[:, c:c + 1], idf[0:32, 0:32]), inc=pe)
        ph.wait("vector", pe)
        ph.op("vector", lambda e: e.tensor_copy(grow[:], PG[0:1, 0:32]), inc=ve)
        ph.wait("scalar", ve)
        ph.op("scalar", lambda e: e.activation(grow[:], grow[:], AF.Exp), inc=ac)
        ph.wait("tensor", ac)
        gvv = grow[:].unsqueeze(2).broadcast_to([1, 32, 64])
        for ns in range(4):
            ph.op("tensor", lambda e, ns=ns, gvv=gvv: e.matmul(
                PF[:, ns * 512:(ns + 1) * 512], onf[0:1, :],
                gvv[:, ns * 8:(ns + 1) * 8, :], start=True, stop=True),
                inc=pe if ns == 3 else None)
        ph.wait("vector", pe)
        ph.op("vector", lambda e: e.tensor_tensor(rin[:], rin[:], PF[:, 0:2048], ALU.mult))
        ph.op("vector", lambda e: e.tensor_copy(rinbf[:], rin[:]), inc=ve)
        for lt in range(2):
            tt = 2 * c + lt
            ph.wait("tensor", ve, ve.v)
            for dc in range(8):
                for ns in range(4):
                    ph.op("tensor", lambda e, dc=dc, ns=ns, lt=lt, par=par: e.matmul(
                        PF[:, ns * 512:(ns + 1) * 512],
                        xfm(par, dc)[:, 4 + lt * 128: 4 + lt * 128 + 128],
                        wz[:, dc * 2048 + ns * 512: dc * 2048 + (ns + 1) * 512],
                        start=(dc == 0), stop=(dc == 7)),
                        inc=pe if (dc == 7 and ns == 3) else None)
            ph.wait("scalar", pe)
            ph.op("scalar", lambda e: e.activation(zsil[:], PF[:, 0:2048], AF.Silu), inc=ac)
            ph.wait("vector", ac)
            yv = ych[:, lt * 2048:(lt + 1) * 2048]
            ph.op("vector", lambda e, yv=yv: e.tensor_mul(ubuf[:], yv, zsil[:]), inc=ve)
            if dbg == "epi" and tt == 21:
                ph.op("vector", lambda e: e.tensor_copy(dbgu[:], ubuf[:]), inc=ve)
            ph.wait("scalar", ve)
            ph.op("scalar", lambda e: e.activation(scre[:], ubuf[:], AF.Square,
                                                   accum_out=lncol[:, 0:1]), inc=ac)
            ph.wait("scalar", ac)
            ph.op("scalar", lambda e: e.activation(
                lncol[:, 2:3], lncol[:, 0:1], AF.Sqrt, scale=1.0 / 2048,
                bias=epsc[:, 0:1]), inc=ac)
            ph.wait("vector", ac)
            ph.op("vector", lambda e: e.reciprocal(lncol[:, 3:4], lncol[:, 2:3]), inc=ve)
            ph.wait("vector", ve)
            ph.op("vector", lambda e: e.scalar_tensor_tensor(
                ubuf[:], ubuf[:], lncol[:, 3:4], nrmb[:], ALU.mult, ALU.mult), inc=ve)
            ph.wait("tensor", ve)
            for half in range(2):
                for f8 in range(8):
                    ph.op("tensor", lambda e, half=half, f8=f8: e.transpose(
                        PTb[:, half * 1024 + f8 * 128: half * 1024 + (f8 + 1) * 128],
                        ubuf[:, half * 1024 + f8 * 128: half * 1024 + (f8 + 1) * 128],
                        idb[:]), inc=pe if f8 == 7 else None)
                ph.wait("vector", pe)
                ph.op("vector", lambda e, half=half: e.tensor_copy(
                    unfm[:, half * 1024:(half + 1) * 1024],
                    PTb[:, half * 1024:(half + 1) * 1024]), inc=ve)
            ph.wait("tensor", ve)
            ph.wait("tensor", ac)
            for ic in range(16):
                for ns in range(2):
                    ph.op("tensor", lambda e, ic=ic, ns=ns: e.matmul(
                        PF[:, ns * 512:(ns + 1) * 512],
                        unfm[:, ic * 128:(ic + 1) * 128],
                        wob[:, ic * 1024 + ns * 512: ic * 1024 + (ns + 1) * 512],
                        start=(ic == 0), stop=(ic == 15)),
                        inc=pe if (ic == 15 and ns == 1) else None)
            ph.wait("vector", pe)
            if dbg == "epi" and tt == 21:
                ph.op("vector", lambda e: e.tensor_copy(scry[:], PF[:, 0:1024]), inc=ve)
            ph.wait("vector", dx, xres_rdy[tt])
            ph.op("vector", lambda e, lt=lt: e.tensor_add(
                sres[:], PF[:, 0:1024], xres[:, lt * D:(lt + 1) * D]), inc=ve)
            ntt = tt + 2
            if ntt < NTT:
                ph.wait("gpsimd", ve)
                xres_rdy[ntt] = ph.dma("gpsimd", xres[:, (ntt % 2) * D:(ntt % 2 + 1) * D],
                                       xin[ntt * 128:(ntt + 1) * 128, :], dx)
            ph.wait("vector", st, st.v)
            ln_emit(ph, sres[:], x1b[:, lt * D:(lt + 1) * D], lncol, scre[:, 0:1024],
                    pe, ve, ac, epsc[:, 0:1],
                    snap=(dbgc[:] if (dbg == "epi" and tt == 21) else None))
            ph.wait("sync", ve)
            ph.dma("sync", x1[tt * 128:(tt + 1) * 128, :], x1b[:, lt * D:(lt + 1) * D], st)
        ph.close()
        if dbg == "epi" and c == 10:
            ph = Ph(ctx)
            for nm, tt_ in (("e_zsil", zsil), ("e_ubuf", ubuf), ("e_unfm", unfm),
                            ("e_sres", sres), ("e_ych", ych), ("e_x1b", x1b),
                            ("e_oproj", scry), ("e_xres", xres), ("e_wob", wob),
                            ("e_dbgu", dbgu), ("e_dbgc", dbgc), ("e_nrmb", nrmb)):
                od_ = nc.dram_tensor(nm, list(tt_[:].shape), F32, kind="ExternalOutput").ap()
                ph.dma("gpsimd", od_, tt_[:], dx)
            ph.wait("gpsimd", dx)
            ph.close()
    eps.close()
    fxs.close()

    ph = Ph(ctx)
    ph.wait("sync", st)
    ph.close()
    return nc, ctx



def _consts_np():
    c = {}
    c["identb"] = np.eye(128, dtype=BF)
    c["identf"] = np.eye(128, dtype=np.float32)
    s = np.arange(128)[:, None]
    l = np.arange(256)[None, :]
    c["maskT"] = (l >= s).astype(BF)
    c["onesf"] = np.ones((1, 128), np.float32)
    return c


def _core_masks(q):
    cm = np.zeros((32, 3), dtype=np.float32)
    m2 = np.zeros((32, 12), dtype=np.float32)
    for i in range(3):
        if i < q:
            cm[:, i] = 1.0
        for j in range(4):
            if i < j < q:
                m2[:, i * 4 + j] = 1.0
    return cm, m2


def prep_l0attn(inputs):
    x = inputs["L0_attn_x"]
    conv_w = np.asarray(inputs["L0_attn_conv_w"], np.float32)
    cdiag = np.zeros((72, 128, 128), np.float32)
    for ot in range(18):
        for k in range(4):
            np.fill_diagonal(cdiag[ot * 4 + k], conv_w[ot * 128:(ot + 1) * 128, k])
    convb = np.ascontiguousarray(
        np.asarray(inputs["L0_attn_conv_b"], np.float32).reshape(18, 128).T)
    cst = _consts_np()
    shared = dict(
        w_in=np.asarray(inputs["L0_attn_W_in"], np.float32),
        cdiag=cdiag, convb=convb,
        dtb=np.asarray(inputs["L0_attn_dt_bias"], np.float32).reshape(32, 1),
        alog=np.asarray(inputs["L0_attn_A_log"], np.float32).reshape(32, 1),
        drep=np.ascontiguousarray(np.tile(
            np.repeat(np.asarray(inputs["L0_attn_D"], np.float32), 64)[None, :],
            (128, 1))),
        nrmrep=np.ascontiguousarray(np.tile(
            np.asarray(inputs["L0_attn_norm_w"], np.float32)[None, :], (128, 1))),
        w_out=np.asarray(inputs["L0_attn_W_out"], np.float32),
        **cst,
    )
    in_maps = []
    for core in range(8):
        b, q = core // 4, core % 4
        t0 = q * T
        xs = np.asarray(x[b, t0:t0 + T], np.float32)
        halo = np.zeros((4, D), np.float32) if q == 0 else np.asarray(
            x[b, t0 - 4:t0], np.float32)
        xT = np.ascontiguousarray(np.concatenate([halo, xs], 0).T)
        cm, m2 = _core_masks(q)
        in_maps.append(dict(shared, xinT=xT, xin=np.ascontiguousarray(xs),
                            cmask=cm, m2=m2))
    return in_maps


def kernel_unfused(**inputs):
    res1 = run_bass_kernel_spmd(build_l0attn()[0], prep_l0attn(inputs),
                                core_ids=list(range(8)))
    x1p = [np.ascontiguousarray(res1.results[c]["x1"]) for c in range(8)]
    res2 = run_bass_kernel_spmd(build_l0ffn()[0], prep_l0ffn(inputs, x1p),
                                core_ids=list(range(8)))
    x2p = [np.ascontiguousarray(res2.results[c]["x2"]) for c in range(8)]
    res3 = run_bass_kernel_spmd(build_l1attn()[0], prep_l1attn(inputs, x2p),
                                core_ids=list(range(8)))
    x3p = [np.ascontiguousarray(res3.results[c]["x3"]) for c in range(8)]
    res4 = run_bass_kernel_spmd(build_l1ffn()[0], prep_l1ffn(inputs, x3p),
                                core_ids=list(range(8)))
    out = np.zeros((2, 16384, D), np.float32)
    for c in range(8):
        b, q = c // 4, c % 4
        out[b, q * T:(q + 1) * T] = res4.results[c]["out"]
    return out


def build_fused():
    nc = bass.Bass("TRN2", target_bir_lowering=False,
                   detect_race_conditions=False)
    ctx = Ctx(nc)
    handoff = {}

    def mkdram(prefix):
        def dram(n, s, d, k):
            if n in ("x1", "x2", "x3"):
                if n not in handoff:
                    handoff[n] = nc.dram_tensor(n, s, d).ap()
                return handoff[n]
            if n == "out":
                return nc.dram_tensor("out", s, d, kind="ExternalOutput").ap()
            return nc.dram_tensor(prefix + n, s, d, kind="ExternalInput").ap()
        return dram

    for bld, pfx in ((build_l0attn, "A_"), (build_l0ffn, "B_"),
                     (build_l1attn, "C_"), (build_l1ffn, "D_")):
        ctx.push_scope()
        bld(env=(nc, ctx, mkdram(pfx)))
        ctx.pop_scope()
    return nc, ctx


def kernel(**inputs):
    nc, _ = build_fused()
    im1 = prep_l0attn(inputs)
    im2 = prep_l0ffn(inputs, None)
    im3 = prep_l1attn(inputs, None)
    im4 = prep_l1ffn(inputs, None)
    in_maps = []
    for c in range(8):
        m = {}
        for pfx, im in (("A_", im1), ("B_", im2), ("C_", im3), ("D_", im4)):
            for k, v in im[c].items():
                if k in ("x1", "x2", "x3"):
                    continue
                m[pfx + k] = v
        in_maps.append(m)
    res = run_bass_kernel_spmd(nc, in_maps, core_ids=list(range(8)))
    out = np.zeros((2, 16384, D), np.float32)
    for c in range(8):
        b, q = c // 4, c % 4
        out[b, q * T:(q + 1) * T] = res.results[c]["out"]
    return out



def top2_emit(ph, ctx, probs, gcols, cols, pe, ve, ac, scale):
    ph.wait("vector", ve)
    ph.op("vector", lambda e: e.reduce_max(cols[:, 0:1], probs, axis=AX.X), inc=ve)
    ph.wait("vector", ve)
    ph.op("vector", lambda e: e.tensor_scalar(gcols, probs, cols[:, 0:1], None, ALU.is_lt), inc=ve)
    ph.wait("vector", ve)
    ph.op("vector", lambda e: e.tensor_mul(gcols, gcols, probs), inc=ve)
    ph.wait("vector", ve)
    ph.op("vector", lambda e: e.reduce_max(cols[:, 1:2], gcols, axis=AX.X), inc=ve)
    ph.wait("vector", ve)
    ph.op("vector", lambda e: e.tensor_scalar(gcols, probs, cols[:, 1:2], None, ALU.is_ge), inc=ve)
    ph.wait("vector", ve)
    ph.op("vector", lambda e: e.tensor_mul(gcols, gcols, probs), inc=ve)
    ph.wait("vector", ve)
    ph.op("vector", lambda e: e.tensor_scalar_mul(gcols, gcols, scale), inc=ve)


def softmax_emit(ph, probs_out, logits_psum, cols, pe, ve, ac):
    ph.op("vector", lambda e: e.reduce_max(cols[:, 0:1], logits_psum, axis=AX.X), inc=ve)
    ph.wait("vector", ve)
    ph.op("vector", lambda e: e.tensor_scalar_mul(cols[:, 1:2], cols[:, 0:1], -1.0), inc=ve)
    ph.wait("scalar", ve)
    ph.op("scalar", lambda e: e.activation(probs_out, logits_psum, AF.Exp,
                                           bias=cols[:, 1:2]), inc=ac)
    ph.wait("vector", ac)
    ph.op("vector", lambda e: e.reduce_sum(cols[:, 2:3], probs_out, axis=AX.X), inc=ve)
    ph.wait("vector", ve)
    ph.op("vector", lambda e: e.reciprocal(cols[:, 3:4], cols[:, 2:3]), inc=ve)
    ph.wait("vector", ve)
    ph.op("vector", lambda e: e.tensor_scalar(
        probs_out, probs_out, cols[:, 3:4], None, ALU.mult), inc=ve)



def build_l0ffn(nexp=18, do_gate=True, env=None):
    nc, ctx, dram = _env(env)
    ei = lambda n, s, d: dram(n, s, d, "in")
    eo = lambda n, s, d: dram(n, s, d, "out")

    x1 = ei("x1", [T, D], BF16)
    wrT = ei("wrT", [18, D, D], F32)
    brf = ei("brf", [1, 18 * D], F32)
    centT = ei("centT", [D, 16], F32)
    ebias = ei("ebias", [1, 16], F32)
    identb = ei("identb", [128, 128], BF16)
    onesbf = ei("onesbf", [1, 128], BF16)
    x2 = eo("x2", [T, D], BF16)

    TG = 1024
    NTG = 4

    idb = ctx.sb("idb", [128, 128], BF16)
    onb = ctx.sb("onb", [1, 128], BF16)
    ctb = ctx.sb("ctb", [128, 8 * 16], BF16)
    ebb = ctx.sb("ebb", [1, 16], BF16)
    brb = ctx.sb("brb", [1, 18 * D], BF16)
    epsc = ctx.sb("epsc", [128, 1], F32)
    gts = ctx.sb("gts", [128, 8 * 18], F32)
    x1tg = ctx.sb("x1tg", [128, 8 * D], BF16)
    ufm = ctx.sb("ufm", [128, 8 * TG], BF16)
    ytg = ctx.sb("ytg", [128, 8 * D], F32)
    web = ctx.sb("web", [128, 2 * 8 * D], BF16)
    sres = ctx.sb("sres", [128, D], F32)
    scre = ctx.sb("scre", [128, D], F32)
    lncol = ctx.sb("lncol", [128, 8], F32)
    prb = ctx.sb("prb", [128, 16], F32)
    x2b = ctx.sb("x2b", [128, 2 * D], BF16)

    PF = ctx.ps("PF", [128, 2048], F32)
    PG = ctx.ps("PG", [128, 1024], F32)
    PTb = ctx.ps("PTb", [128, 2048], BF16)

    dw = ctx.sem("dw")
    dx = ctx.sem("dx")
    ds = ctx.sem("ds")
    pe = ctx.sem("pe")
    ve = ctx.sem("ve")
    ac = ctx.sem("ac")
    st = ctx.sem("st")

    ph = Ph(ctx)
    ph.dma("gpsimd", ctb[:].rearrange("p (dc o) -> p dc o", dc=8),
           centT.rearrange("(dc p) o -> p dc o", p=128), dw)
    ph.dma("gpsimd", ebb[:], ebias, dw)
    ph.dma("gpsimd", brb[:], brf, dw)
    ph.dma("sync", idb[:], identb, ds)
    ph.dma("sync", onb[:], onesbf, ds)
    for dc in range(8):
        ph.dma("gpsimd", web[:, dc * D:(dc + 1) * D],
               wrT[0, dc * 128:(dc + 1) * 128, :], dw)
    ph.wait("vector", ds)
    ph.op("vector", lambda e: e.memset(epsc[:], 1e-5), inc=ve)
    ph.close()

    for tg in range(4):
        ph = Ph(ctx)
        xw = ds.v
        for t8 in range(8):
            xw = ph.dma("sync", x1tg[:, t8 * D:(t8 + 1) * D],
                        x1[(tg * 8 + t8) * 128:(tg * 8 + t8 + 1) * 128, :], ds)
        ph.wait("tensor", ds, xw)
        if tg == 0:
            ph.wait("tensor", dw, dw.v)
        for t8 in range(8):
            for half in range(2):
                for f8 in range(4):
                    ph.op("tensor", lambda e, t8=t8, half=half, f8=f8: e.transpose(
                        PTb[:, half * 1024 + f8 * 128: half * 1024 + (f8 + 1) * 128],
                        x1tg[:, t8 * D + half * 512 + f8 * 128: t8 * D + half * 512 + (f8 + 1) * 128],
                        idb[:]), inc=pe if f8 == 3 else None)
                ph.wait("vector", pe)
                for f8 in range(4):
                    fidx = half * 4 + f8
                    ph.op("vector", lambda e, t8=t8, fidx=fidx, half=half, f8=f8: e.tensor_copy(
                        ufm[:, fidx * TG + t8 * 128: fidx * TG + (t8 + 1) * 128],
                        PTb[:, half * 1024 + f8 * 128: half * 1024 + (f8 + 1) * 128]),
                        inc=ve if f8 == 3 else None)
                ph.wait("tensor", ve)
        for t8 in range(8 if do_gate else 0):
            ph.wait("tensor", ve, ve.v)
            for dc in range(8):
                ph.op("tensor", lambda e, t8=t8, dc=dc: e.matmul(
                    PG[:, 0:16], ufm[:, dc * TG + t8 * 128: dc * TG + (t8 + 1) * 128],
                    ctb[:, dc * 16:(dc + 1) * 16], start=(dc == 0), stop=False))
            ph.op("tensor", lambda e, t8=t8: e.matmul(
                PG[:, 0:16], onb[0:1, 0:128], ebb[0:1, :], start=False, stop=True), inc=pe)
            ph.wait("vector", pe)
            ph.wait("scalar", pe)
            softmax_emit(ph, prb[:], PG[:, 0:16], lncol, pe, ve, ac)
            ph.wait("vector", ve)
            top2_emit(ph, ctx, prb[:], gts[:, t8 * 18:t8 * 18 + 16], lncol[:, 4:8],
                      pe, ve, ac, 1.0)
            ph.op("vector", lambda e, t8=t8: e.memset(gts[:, t8 * 18 + 16:t8 * 18 + 18], 0.5), inc=ve)
        ph.op("vector", lambda e: e.memset(ytg[:], 0.0), inc=ve)
        ph.close()

        for e_i in range(nexp):
            ph = Ph(ctx)
            par = e_i % 2
            wcur = lambda dc, par=par: web[:, par * 8 * D + dc * D:(par) * 8 * D + (dc + 1) * D]
            wv = dw.v
            ne, ntg = (e_i + 1, tg) if e_i + 1 < nexp else (0, tg + 1)
            if ntg < 4:
                for dc in range(8):
                    ph.dma("gpsimd", web[:, (1 - par) * 8 * D + dc * D:(1 - par) * 8 * D + (dc + 1) * D],
                           wrT[ne, dc * 128:(dc + 1) * 128, :], dw)
            ph.wait("tensor", dw, wv)
            ev_e = {}
            for t8 in range(8):
                bank = (t8 % 2) * 1024
                if t8 >= 2:
                    ph.wait("tensor", ve, ev_e[t8 - 2])
                for dc in range(8):
                    for ns in range(2):
                        ph.op("tensor", lambda e, t8=t8, dc=dc, ns=ns, bank=bank, par=par: e.matmul(
                            PF[:, bank + ns * 512: bank + (ns + 1) * 512],
                            ufm[:, dc * TG + t8 * 128: dc * TG + (t8 + 1) * 128],
                            wcur(dc)[:, ns * 512:(ns + 1) * 512],
                            start=(dc == 0), stop=False))
                for ns in range(2):
                    ph.op("tensor", lambda e, ns=ns, bank=bank, e_i=e_i: e.matmul(
                        PF[:, bank + ns * 512: bank + (ns + 1) * 512],
                        onb[0:1, 0:128],
                        brb[0:1, e_i * D + ns * 512: e_i * D + (ns + 1) * 512],
                        start=False, stop=True), inc=pe if ns == 1 else None)
                ph.wait("vector", pe)
                gcol = gts[:, t8 * 18 + e_i: t8 * 18 + e_i + 1]
                ph.op("vector", lambda e, t8=t8, bank=bank, gcol=gcol: e.scalar_tensor_tensor(
                    ytg[:, t8 * D:(t8 + 1) * D], PF[:, bank:bank + 1024], gcol,
                    ytg[:, t8 * D:(t8 + 1) * D], ALU.mult, ALU.add), inc=ve)
                ev_e[t8] = ve.v
            ph.close()

        ph = Ph(ctx)
        for t8 in range(8):
            tt = tg * 8 + t8
            ph.wait("vector", ac)
            ph.op("vector", lambda e, t8=t8: e.scalar_tensor_tensor(
                sres[:], x1tg[:, t8 * D:(t8 + 1) * D], 2.0,
                ytg[:, t8 * D:(t8 + 1) * D], ALU.mult, ALU.add), inc=ve)
            ph.wait("vector", st, st.v)
            ln_emit(ph, sres[:], x2b[:, (t8 % 2) * D:(t8 % 2 + 1) * D], lncol, scre[:],
                    pe, ve, ac, epsc[:, 0:1])
            ph.wait("sync", ve)
            ph.dma("sync", x2[tt * 128:(tt + 1) * 128, :], x2b[:, (t8 % 2) * D:(t8 % 2 + 1) * D], st)
        ph.close()

    ph = Ph(ctx)
    ph.wait("sync", st)
    ph.close()
    return nc, ctx


def prep_l0ffn(inputs, x1_parts):
    Wr = np.asarray(inputs["L0_ffn_Wr"], np.float32)
    Ws = np.asarray(inputs["L0_ffn_Ws"], np.float32)
    br = np.asarray(inputs["L0_ffn_br"], np.float32)
    bs = np.asarray(inputs["L0_ffn_bs"], np.float32)
    wrT = np.concatenate([Wr.transpose(0, 2, 1), Ws.transpose(0, 2, 1)], 0)
    wrT = np.ascontiguousarray(wrT)
    brf = np.concatenate([br, bs], 0).reshape(1, 18 * D)
    centT = np.ascontiguousarray(np.asarray(inputs["L0_ffn_centroids"], np.float32).T)
    ebias = np.asarray(inputs["L0_ffn_expert_biases"], np.float32).reshape(1, 16)
    shared = dict(wrT=wrT, brf=np.ascontiguousarray(brf), centT=centT, ebias=ebias,
                  identb=np.eye(128, dtype=BF), onesbf=np.ones((1, 128), BF))
    if x1_parts is None:
        return [dict(shared) for c in range(8)]
    return [dict(shared, x1=x1_parts[c]) for c in range(8)]



def build_l1attn(mode='full', nchunk=8, qk=True, chop=99, env=None):
    nc, ctx, dram = _env(env)
    ei = lambda n, s, d: dram(n, s, d, "in")
    eo = lambda n, s, d: dram(n, s, d, "out")

    x2 = ei("x2", [T, D], BF16)
    wqT = ei("wqT", [D, D], F32)
    wkT = ei("wkT", [D, D], F32)
    wvT = ei("wvT", [D, D], F32)
    woT = ei("woT", [D, D], F32)
    bq_pt = ei("bq_pt", [128, 8], F32)
    bk_pt = ei("bk_pt", [128, 8], F32)
    bv_row = ei("bv_row", [1, D], F32)
    bo_row = ei("bo_row", [1, D], F32)
    bk_row = ei("bk_row", [1, D], F32)
    identb = ei("identb", [128, 128], BF16)
    onesbf = ei("onesbf", [1, 128], BF16)
    ones128 = ei("ones128", [128, 1], BF16)
    bones = ei("bones", [128, 2], BF16)
    cm128 = ei("cm128", [128, 3], F32)
    trimask = ei("trimask", [128, 128], BF16)
    identf2 = ei("identf2", [128, 128], F32)
    x3 = eo("x3", [T, D], BF16)

    kcd = nc.dram_tensor("kcd", [1, 1024], F32).ap()
    agsrc = nc.dram_tensor("agsrc2", [128, 520], F32).ap()
    agdst = nc.dram_tensor("agdst2", [512, 520], F32).ap()

    TG = 1024

    idb = ctx.sb("idb", [128, 128], BF16)
    onb = ctx.sb("onb", [1, 128], BF16)
    on128 = ctx.sb("on128", [128, 1], BF16)
    bon = ctx.sb("bon", [128, 2], BF16)
    cmb = ctx.sb("cmb", [128, 3], F32)
    epsc = ctx.sb("epsc", [128, 1], F32)
    wqtb = ctx.sb("wqtb", [128, 8 * D], BF16)
    wktb = ctx.sb("wktb", [128, 8 * D], BF16)
    wvtb = ctx.sb("wvtb", [128, 8 * D], BF16)
    wotb = ctx.sb("wotb", [128, 8 * D], BF16)
    bqc = ctx.sb("bqc", [128, 8], F32)
    bkc = ctx.sb("bkc", [128, 8], F32)
    bvr = ctx.sb("bvr", [1, D], BF16)
    bor = ctx.sb("bor", [1, D], BF16)
    bkr = ctx.sb("bkr", [1, D], BF16)
    x2tg = ctx.sb("x2tg", [128, 8 * D], BF16)
    xfm = ctx.sb("xfm", [128, 8 * TG], BF16)
    qfm = ctx.sb("qfm", [128, 8 * TG], BF16)
    kfm = ctx.sb("kfm", [128, 8 * TG], BF16)
    ktm = ctx.sb("ktm", [128, D], BF16)
    vtm = ctx.sb("vtm", [128, D], BF16)
    mtb2 = ctx.sb("mtb2", [128, 16 * 128], BF16)
    atm = ctx.sb("atm", [128, D], BF16)
    attfm = ctx.sb("attfm", [128, D], BF16)
    ssb = ctx.sb("ssb", [128, 512], F32)
    srbf = ctx.sb("srbf", [128, 512], BF16)
    kcr = ctx.sb("kcr", [128, 8], F32)
    kcs = ctx.sb("kcs", [128, 8 * 128], F32)
    kcrow = ctx.sb("kcrow", [1, D], F32)
    pq = ctx.sb("pq", [128, 128], BF16)
    denb = ctx.sb("denb", [128, 16], F32)
    rden = ctx.sb("rden", [128, 16], F32)
    scr = ctx.sb("scr", [128, D], F32)
    scrq = ctx.sb("scrq", [128, D], F32)
    sres = ctx.sb("sres", [128, D], F32)
    scre = ctx.sb("scre", [128, D], F32)
    lncol = ctx.sb("lncol", [128, 8], F32)
    x3b = ctx.sb("x3b", [128, 2 * D], BF16)
    fld = ctx.sb("fld", [128, 520], F32)
    trib = ctx.sb("trib", [128, 128], BF16)
    idf2 = ctx.sb("idf2", [128, 128], F32)
    zz128 = ctx.sb("zz128", [128, 128], F32)

    PF = ctx.ps("PF", [128, 2048], F32)
    PG = ctx.ps("PG", [128, 1024], F32)
    PTb = ctx.ps("PTb", [128, 2048], BF16)

    dw = ctx.sem("dw")
    dx = ctx.sem("dx")
    ds = ctx.sem("ds")
    pe = ctx.sem("pe")
    ve = ctx.sem("ve")
    ac = ctx.sem("ac")
    st = ctx.sem("st")
    cc = ctx.sem("cc")

    ph = Ph(ctx)
    for w, t_ in ((wqT, wqtb), (wkT, wktb), (wvT, wvtb), (woT, wotb)):
        for dc in range(8):
            ph.dma("gpsimd", t_[:, dc * D:(dc + 1) * D],
                   w[dc * 128:(dc + 1) * 128, :], dw)
    ph.dma("gpsimd", bvr[:], bv_row, dw)
    ph.dma("gpsimd", bor[:], bo_row, dw)
    ph.dma("gpsimd", bkr[:], bk_row, dw)
    for dst, srcw in ((idb, identb), (onb, onesbf), (on128, ones128), (bon, bones),
                      (cmb, cm128), (bqc, bq_pt), (bkc, bk_pt), (trib, trimask),
                      (idf2, identf2)):
        ph.dma("sync", dst[:], srcw, ds)
    ph.wait("vector", ds)
    ph.op("vector", lambda e: e.memset(epsc[:], 1e-5))
    ph.op("vector", lambda e: e.memset(ssb[:], 0.0))
    ph.op("vector", lambda e: e.memset(kcr[:], 0.0))
    ph.op("vector", lambda e: e.memset(kcrow[:], 0.0))
    ph.op("vector", lambda e: e.memset(zz128[:], 0.0), inc=ve)
    ph.close()

    def load_transpose_xtg(ph, tg):
        xw = ds.v
        for t8 in range(8):
            xw = ph.dma("sync", x2tg[:, t8 * D:(t8 + 1) * D],
                        x2[(tg * 8 + t8) * 128:(tg * 8 + t8 + 1) * 128, :], ds)
        ph.wait("tensor", ds, xw)
        for t8 in range(8):
            for half in range(2):
                for f8 in range(4):
                    ph.op("tensor", lambda e, t8=t8, half=half, f8=f8: e.transpose(
                        PTb[:, half * 1024 + f8 * 128: half * 1024 + (f8 + 1) * 128],
                        x2tg[:, t8 * D + half * 512 + f8 * 128: t8 * D + half * 512 + (f8 + 1) * 128],
                        idb[:]), inc=pe if f8 == 3 else None)
                ph.wait("vector", pe)
                for f8 in range(4):
                    fidx = half * 4 + f8
                    ph.op("vector", lambda e, t8=t8, fidx=fidx, half=half, f8=f8: e.tensor_copy(
                        xfm[:, fidx * TG + t8 * 128: fidx * TG + (t8 + 1) * 128],
                        PTb[:, half * 1024 + f8 * 128: half * 1024 + (f8 + 1) * 128]),
                        inc=ve if f8 == 3 else None)
                ph.wait("tensor", ve)

    lastu = {}

    def emit_tm_proj(ph, t8, wtb, brow, dst, elu, base=1024):
        ph.wait("tensor", ve, lastu.get(base, 0))
        for dc in range(8):
            for ns in range(2):
                ph.op("tensor", lambda e, dc=dc, ns=ns, t8=t8, wtb=wtb, base=base: e.matmul(
                    PF[:, base + ns * 512: base + (ns + 1) * 512],
                    xfm[:, dc * TG + t8 * 128: dc * TG + (t8 + 1) * 128],
                    wtb[:, dc * D + ns * 512: dc * D + (ns + 1) * 512],
                    start=(dc == 0), stop=False))
        for ns in range(2):
            ph.op("tensor", lambda e, ns=ns, brow=brow, base=base: e.matmul(
                PF[:, base + ns * 512: base + (ns + 1) * 512],
                onb[0:1, 0:128], brow[0:1, ns * 512:(ns + 1) * 512],
                start=False, stop=True), inc=pe if ns == 1 else None)
        ph.wait("vector", pe)
        if elu:
            ph.op("vector", lambda e, base=base: e.tensor_scalar_min(
                scr[:], PF[:, base:base + 1024], 0.0), inc=ve)
            ph.wait("scalar", ve)
            ph.op("scalar", lambda e: e.activation(scr[:], scr[:], AF.Exp), inc=ac)
            ph.wait("vector", ac)
            ph.op("vector", lambda e, dst=dst, base=base: e.scalar_tensor_tensor(
                dst, PF[:, base:base + 1024], 0.0, scr[:], ALU.max, ALU.add), inc=ve)
        else:
            ph.op("vector", lambda e, dst=dst, base=base: e.tensor_copy(
                dst, PF[:, base:base + 1024]), inc=ve)
        lastu[base] = ve.v

    for tg in range(4 if mode != 'p2' else 0):
        ph = Ph(ctx)
        load_transpose_xtg(ph, tg)
        for t8 in range(8):
            gt = tg * 8 + t8
            emit_tm_proj(ph, t8, wktb, bkr, ktm[:], elu=True)
            emit_tm_proj(ph, t8, wvtb, bvr, vtm[:], elu=False, base=0)
            ph.wait("tensor", ve)
            for ns in range(2):
                ph.op("tensor", lambda e, ns=ns: e.matmul(
                    PG[0:1, ns * 512:(ns + 1) * 512], on128[:, 0:1],
                    ktm[:, ns * 512:(ns + 1) * 512],
                    start=True, stop=True), inc=pe if ns == 1 else None)
            ph.wait("vector", pe)
            ph.op("vector", lambda e: e.tensor_add(kcrow[:], kcrow[:], PG[0:1, 0:1024]), inc=ve)
            for h in range(16):
                ph.op("tensor", lambda e, h=h: e.matmul(
                    PF[(h % 2) * 64:(h % 2) * 64 + 64, (h // 2) * 64:(h // 2) * 64 + 64],
                    ktm[:, h * 64:(h + 1) * 64], vtm[:, h * 64:(h + 1) * 64],
                    start=True, stop=True), inc=pe if h == 15 else None)
            ph.wait("vector", pe)
            ph.op("vector", lambda e: e.tensor_add(ssb[:], ssb[:], PF[:, 0:512]), inc=ve)
            lastu[0] = ve.v
        ph.close()

    if mode == 'p2':
        ph = Ph(ctx)
        ph.op("vector", lambda e: e.memset(srbf[:], 0.0), inc=ve)
        ph.close()
    ph = Ph(ctx) if mode != 'p2' else None
    if ph is not None:
        ph.wait("sync", ve)
        ph.dma("sync", kcd, kcrow[:], ds)
        ph.wait("sync", ds)
        ph.dma("sync", kcr[:], kcd.rearrange("o (f p) -> (o p) f", p=128), ds,
               allow_slow_non_contiguous=True)
        ph.wait("sync", ds)
        ph.dma("sync", agsrc[:, 0:512], ssb[:], st)
        ph.dma("sync", agsrc[:, 512:520], kcr[:], st)
        ph.wait("gpsimd", st)
        ph.op("gpsimd", lambda e: e.collective_compute(
            "AllGather", ALU.bypass, replica_groups=RG,
            ins=[agsrc.opt()], outs=[agdst.opt()]), inc=cc)
        ph.close()

    ph = Ph(ctx)
    if mode != 'p2':
        ph.wait("sync", cc, cc.v)
    ph.op("vector", lambda e: e.memset(ssb[:], 0.0))
    ph.op("vector", lambda e: e.memset(kcr[:], 0.0), inc=ve)
    for i in range(3 if mode != 'p2' else 0):
        if i > 0:
            ph.wait("sync", ve, ve.v)
        ph.dma("sync", fld[:], agdst[i * 128:(i + 1) * 128, :], ds)
        ph.wait("vector", ds)
        ph.op("vector", lambda e, i=i: e.tensor_scalar(
            scr[:, 0:520], fld[:], cmb[:, i:i + 1], None, ALU.mult), inc=ve)
        ph.wait("vector", ve)
        ph.op("vector", lambda e: e.tensor_add(ssb[:], ssb[:], scr[:, 0:512]))
        ph.op("vector", lambda e: e.tensor_add(kcr[:], kcr[:], scr[:, 512:520]), inc=ve)
    ph.op("vector", lambda e: e.tensor_copy(srbf[:], ssb[:]), inc=ve)
    ph.close()

    for tg in range(4 if mode != 'p1' else 0):
        ph = Ph(ctx)
        load_transpose_xtg(ph, tg)
        qkval = {}
        qit = 0
        for (wtb, bcol, dstb) in (((wqtb, bqc, qfm), (wktb, bkc, kfm)) if qk else ()):
            for oc in range(8):
                for tns in range(2):
                    qp = qit % 2
                    pqk = PF[:, 1024 + qp * 512: 1024 + qp * 512 + 512]
                    sqk = scr[:, qp * 512: qp * 512 + 512]
                    dqk = scrq[:, qp * 512: qp * 512 + 512]
                    ph.wait("tensor", ve, qkval.get(qit - 2, 0))
                    for dc in range(8):
                        ph.op("tensor", lambda e, oc=oc, tns=tns, dc=dc, wtb=wtb, pqk=pqk: e.matmul(
                            pqk,
                            wtb[:, dc * D + oc * 128: dc * D + oc * 128 + 128],
                            xfm[:, dc * TG + tns * 512: dc * TG + (tns + 1) * 512],
                            start=(dc == 0), stop=(dc == 7)), inc=pe if dc == 7 else None)
                    ph.wait("vector", pe)
                    ph.op("vector", lambda e, oc=oc, bcol=bcol, pqk=pqk, sqk=sqk: e.tensor_scalar(
                        sqk, pqk, bcol[:, oc:oc + 1], 0.0, ALU.add, ALU.min), inc=ve)
                    ph.wait("scalar", ve)
                    ph.op("scalar", lambda e, sqk=sqk: e.activation(sqk, sqk, AF.Exp), inc=ac)
                    ph.op("vector", lambda e, oc=oc, bcol=bcol, pqk=pqk, dqk=dqk: e.tensor_scalar(
                        dqk, pqk, bcol[:, oc:oc + 1], 0.0, ALU.add, ALU.max), inc=ve)
                    qkval[qit] = ve.v
                    ph.wait("vector", ac)
                    ph.wait("vector", ve)
                    ph.op("vector", lambda e, oc=oc, tns=tns, dstb=dstb, sqk=sqk, dqk=dqk: e.tensor_add(
                        dstb[:, oc * TG + tns * 512: oc * TG + (tns + 1) * 512],
                        sqk, dqk), inc=ve)
                    qit += 1
        if qk:
            lastu[0] = lastu[1024] = ve.v
        ph.close()

        for t8 in range(nchunk):
            ph = Ph(ctx)
            emit_tm_proj(ph, t8, wktb, bkr, ktm[:], elu=True)
            emit_tm_proj(ph, t8, wvtb, bvr, vtm[:], elu=False, base=0)
            if chop < 1:
                ph.close()
                continue
            hsl = lambda buf, h, t8=t8: buf[(h % 2) * 64:(h % 2) * 64 + 64,
                                            (h // 2) * TG + t8 * 128:(h // 2) * TG + (t8 + 1) * 128]
            ph.wait("tensor", ve)
            horder = list(range(0, 16, 2)) + list(range(1, 16, 2))
            for hi, h in enumerate(horder):
                ph.op("tensor", lambda e, h=h: e.matmul(
                    PF[:, h * 128:(h + 1) * 128], hsl(kfm, h), hsl(qfm, h),
                    start=True, stop=True), inc=pe if hi == 15 else None)
            ph.wait("vector", pe)
            for h in range(16):
                ph.op("vector", lambda e, h=h: e.tensor_tensor(
                    mtb2[:, h * 128:(h + 1) * 128], PF[:, h * 128:(h + 1) * 128],
                    trib[:], ALU.mult), inc=ve if h == 15 else None)
            lastu[0] = lastu[1024] = ve.v
            if chop < 2:
                ph.close()
                continue
            for f8 in range(8):
                ph.op("vector", lambda e, f8=f8, t8=t8: e.tensor_tensor_scan(
                    kcs[:, f8 * 128:(f8 + 1) * 128],
                    kfm[:, f8 * TG + t8 * 128: f8 * TG + (t8 + 1) * 128],
                    zz128[:], kcr[:, f8:f8 + 1], ALU.add, ALU.add), inc=ve)
                ph.wait("vector", ve)
                ph.op("vector", lambda e, f8=f8: e.tensor_copy(
                    kcr[:, f8:f8 + 1], kcs[:, f8 * 128 + 127:(f8 + 1) * 128]), inc=ve)
                ph.wait("vector", ve)
                ph.op("vector", lambda e, f8=f8, t8=t8: e.tensor_tensor(
                    pq[:], qfm[:, f8 * TG + t8 * 128: f8 * TG + (t8 + 1) * 128],
                    kcs[:, f8 * 128:(f8 + 1) * 128], ALU.mult), inc=ve)
                ph.wait("tensor", ve)
                if f8 == 0:
                    ph.wait("tensor", ac, ac.v)
                ph.op("tensor", lambda e, f8=f8: e.matmul(
                    PG[:, 512 + f8 * 2:512 + f8 * 2 + 2], pq[:], bon[:],
                    start=True, stop=True), inc=pe)
            ph.wait("vector", pe)
            if chop < 3:
                ph.close()
                continue
            ph.op("vector", lambda e: e.tensor_scalar_max(denb[:], PG[:, 512:528], 1e-6), inc=ve)
            ph.wait("vector", ve)
            ph.op("vector", lambda e: e.reciprocal(rden[:], denb[:]), inc=ve)
            if chop < 4:
                ph.close()
                continue
            ph.wait("tensor", ve)
            for h in range(16):
                ph.op("tensor", lambda e, h=h: e.matmul(
                    PF[:, h * 64:(h + 1) * 64], mtb2[:, h * 128:(h + 1) * 128],
                    vtm[:, h * 64:(h + 1) * 64], start=True, stop=False))
                ph.op("tensor", lambda e, h=h: e.matmul(
                    PF[:, h * 64:(h + 1) * 64], hsl(qfm, h),
                    srbf[(h % 2) * 64:(h % 2) * 64 + 64, (h // 2) * 64:(h // 2) * 64 + 64],
                    start=False, stop=True), inc=pe if h == 15 else None)
            ph.wait("vector", pe)
            if chop < 5:
                ph.close()
                continue
            rdv = rden[:].unsqueeze(2).broadcast_to([128, 16, 64])
            ph.op("vector", lambda e, rdv=rdv: e.tensor_tensor(
                atm[:].rearrange("p (h q) -> p h q", h=16),
                PF[:, 0:1024].rearrange("p (h q) -> p h q", h=16), rdv, ALU.mult), inc=ve)
            lastu[0] = ve.v
            if chop < 6:
                ph.close()
                continue
            for h in range(16):
                ph.op("tensor", lambda e, h=h: e.matmul(
                    PG[(h % 2) * 64:(h % 2) * 64 + 64, (h // 2) * 64:(h // 2) * 64 + 64],
                    ktm[:, h * 64:(h + 1) * 64], vtm[:, h * 64:(h + 1) * 64],
                    start=True, stop=True), inc=pe if h == 15 else None)
            ph.wait("vector", pe)
            ph.op("vector", lambda e: e.tensor_add(ssb[:], ssb[:], PG[:, 0:512]), inc=ve)
            ph.wait("vector", ve)
            ph.op("vector", lambda e: e.tensor_copy(srbf[:], ssb[:]), inc=ve)
            if chop < 7:
                ph.close()
                continue
            ph.wait("tensor", ve)
            for f8 in range(8):
                ph.op("tensor", lambda e, f8=f8: e.transpose(
                    PTb[:, f8 * 128:(f8 + 1) * 128], atm[:, f8 * 128:(f8 + 1) * 128],
                    idb[:]), inc=pe if f8 == 7 else None)
            ph.wait("vector", pe)
            ph.op("vector", lambda e: e.tensor_copy(attfm[:], PTb[:, 0:1024]), inc=ve)
            if chop < 8:
                ph.close()
                continue
            ph.wait("tensor", ve)
            for ic in range(8):
                for ns in range(2):
                    ph.op("tensor", lambda e, ic=ic, ns=ns: e.matmul(
                        PF[:, 1024 + ns * 512:1024 + (ns + 1) * 512],
                        attfm[:, ic * 128:(ic + 1) * 128],
                        wotb[:, ic * D + ns * 512: ic * D + (ns + 1) * 512],
                        start=(ic == 0), stop=False))
            for ns in range(2):
                ph.op("tensor", lambda e, ns=ns: e.matmul(
                    PF[:, 1024 + ns * 512:1024 + (ns + 1) * 512],
                    onb[0:1, 0:128], bor[0:1, ns * 512:(ns + 1) * 512],
                    start=False, stop=True), inc=pe if ns == 1 else None)
            ph.wait("vector", pe)
            ph.wait("vector", ac)
            ph.op("vector", lambda e, t8=t8: e.tensor_add(
                sres[:], PF[:, 1024:2048], x2tg[:, t8 * D:(t8 + 1) * D]), inc=ve)
            lastu[1024] = ve.v
            ph.wait("vector", st, st.v)
            ln_emit(ph, sres[:], x3b[:, (t8 % 2) * D:(t8 % 2 + 1) * D], lncol, scre[:],
                    pe, ve, ac, epsc[:, 0:1])
            ph.wait("sync", ve)
            tt = tg * 8 + t8
            ph.dma("sync", x3[tt * 128:(tt + 1) * 128, :], x3b[:, (t8 % 2) * D:(t8 % 2 + 1) * D], st)
            ph.close()

    ph = Ph(ctx)
    if mode == 'p1':
        ph.dma("gpsimd", x3[0:128, 0:8], kcr[:], st)
    ph.wait("sync", st)
    ph.close()
    return nc, ctx


def prep_l1attn(inputs, x2_parts):
    tM = np.tril(np.ones((128, 128), np.float32))
    s_ = np.arange(128)[:, None]
    l_ = np.arange(128)[None, :]
    trimask = (l_ >= s_).astype(BF)
    bones = np.zeros((128, 2), BF)
    bones[0:64, 0] = 1
    bones[64:128, 1] = 1
    shared = dict(
        wqT=np.ascontiguousarray(np.asarray(inputs["L1_attn_Wq"], np.float32).T),
        wkT=np.ascontiguousarray(np.asarray(inputs["L1_attn_Wk"], np.float32).T),
        wvT=np.ascontiguousarray(np.asarray(inputs["L1_attn_Wv"], np.float32).T),
        woT=np.ascontiguousarray(np.asarray(inputs["L1_attn_Wo"], np.float32).T),
        bq_pt=np.ascontiguousarray(np.asarray(inputs["L1_attn_bq"], np.float32).reshape(8, 128).T),
        bk_pt=np.ascontiguousarray(np.asarray(inputs["L1_attn_bk"], np.float32).reshape(8, 128).T),
        bv_row=np.asarray(inputs["L1_attn_bv"], np.float32).reshape(1, D),
        bo_row=np.asarray(inputs["L1_attn_bo"], np.float32).reshape(1, D),
        bk_row=np.asarray(inputs["L1_attn_bk"], np.float32).reshape(1, D),
        identb=np.eye(128, dtype=BF), onesbf=np.ones((1, 128), BF),
        ones128=np.ones((128, 1), BF), bones=bones, trimask=trimask,
        identf2=np.eye(128, dtype=np.float32),
    )
    out = []
    for c in range(8):
        q = c % 4
        cm = np.zeros((128, 3), np.float32)
        for i in range(3):
            if i < q:
                cm[:, i] = 1.0
        out.append(dict(shared, cm128=cm) if x2_parts is None else dict(shared, x2=x2_parts[c], cm128=cm))
    return out



def build_l1ffn(env=None):
    nc, ctx, dram = _env(env)
    ei = lambda n, s, d: dram(n, s, d, "in")
    eo = lambda n, s, d: dram(n, s, d, "out")

    x3 = ei("x3", [T, D], BF16)
    gwT = ei("gwT", [D, 8], F32)
    w1e = ei("w1e", [9, D, 4096], F32)
    w3e = ei("w3e", [9, D, 4096], F32)
    w2e = ei("w2e", [9, 4096, D], F32)
    identb = ei("identb", [128, 128], BF16)
    out = eo("out", [T, D], F32)

    TG = 1024

    idb = ctx.sb("idb", [128, 128], BF16)
    gwb = ctx.sb("gwb", [128, 8 * 8], BF16)
    epsc = ctx.sb("epsc", [128, 1], F32)
    gts = ctx.sb("gts", [128, 8 * 9], F32)
    x3tg = ctx.sb("x3tg", [128, 8 * D], BF16)
    ufm = ctx.sb("ufm", [128, 8 * TG], BF16)
    ytg = ctx.sb("ytg", [128, 8 * D], F32)
    hbuf = ctx.sb("hbuf", [128, 16 * TG], BF16)
    w13b = ctx.sb("w13b", [128, 2 * 2 * 4096], BF16)
    w2b = ctx.sb("w2b", [128, 16 * D], BF16)
    prb = ctx.sb("prb", [128, 8], F32)
    scr = ctx.sb("scr", [128, 512], F32)
    sres = ctx.sb("sres", [128, D], F32)
    scre = ctx.sb("scre", [128, D], F32)
    lncol = ctx.sb("lncol", [128, 8], F32)
    outb = ctx.sb("outb", [128, 2 * D], F32)

    PF = ctx.ps("PF", [128, 2048], F32)
    PG = ctx.ps("PG", [128, 1024], F32)
    PTb = ctx.ps("PTb", [128, 2048], BF16)

    dw = ctx.sem("dw")
    ds = ctx.sem("ds")
    pe = ctx.sem("pe")
    ve = ctx.sem("ve")
    ac = ctx.sem("ac")
    st = ctx.sem("st")

    ph = Ph(ctx)
    ph.dma("gpsimd", gwb[:].rearrange("p (dc o) -> p dc o", dc=8),
           gwT.rearrange("(dc p) o -> p dc o", p=128), dw)
    ph.dma("sync", idb[:], identb, ds)
    for dc in range(8):
        ph.dma("gpsimd", w13b[:, dc * 512:(dc + 1) * 512],
               w1e[0, dc * 128:(dc + 1) * 128, 0:512], dw)
        ph.dma("gpsimd", w13b[:, 4096 + dc * 512:4096 + (dc + 1) * 512],
               w3e[0, dc * 128:(dc + 1) * 128, 0:512], dw)
    ph.wait("vector", ds)
    ph.op("vector", lambda e: e.memset(epsc[:], 1e-5), inc=ve)
    ph.close()

    w13rd = {}
    w13rd[(0, 0, 0)] = dw.v

    def prefetch_w13(ph, tg, e_i, fs, par):
        for dc in range(8):
            ph.dma("gpsimd", w13b[:, par * 8192 + dc * 512: par * 8192 + (dc + 1) * 512],
                   w1e[e_i, dc * 128:(dc + 1) * 128, fs * 512:(fs + 1) * 512], dw)
            ph.dma("gpsimd", w13b[:, par * 8192 + 4096 + dc * 512: par * 8192 + 4096 + (dc + 1) * 512],
                   w3e[e_i, dc * 128:(dc + 1) * 128, fs * 512:(fs + 1) * 512], dw)
        w13rd[(tg, e_i, fs)] = dw.v

    for tg in range(4):
        ph = Ph(ctx)
        xw = ds.v
        for t8 in range(8):
            xw = ph.dma("sync", x3tg[:, t8 * D:(t8 + 1) * D],
                        x3[(tg * 8 + t8) * 128:(tg * 8 + t8 + 1) * 128, :], ds)
        ph.wait("tensor", ds, xw)
        if tg == 0:
            ph.wait("tensor", dw, dw.v)
        for t8 in range(8):
            for half in range(2):
                for f8 in range(4):
                    ph.op("tensor", lambda e, t8=t8, half=half, f8=f8: e.transpose(
                        PTb[:, half * 1024 + f8 * 128: half * 1024 + (f8 + 1) * 128],
                        x3tg[:, t8 * D + half * 512 + f8 * 128: t8 * D + half * 512 + (f8 + 1) * 128],
                        idb[:]), inc=pe if f8 == 3 else None)
                ph.wait("vector", pe)
                for f8 in range(4):
                    fidx = half * 4 + f8
                    ph.op("vector", lambda e, t8=t8, fidx=fidx, half=half, f8=f8: e.tensor_copy(
                        ufm[:, fidx * TG + t8 * 128: fidx * TG + (t8 + 1) * 128],
                        PTb[:, half * 1024 + f8 * 128: half * 1024 + (f8 + 1) * 128]),
                        inc=ve if f8 == 3 else None)
                ph.wait("tensor", ve)
        for t8 in range(8):
            ph.wait("tensor", ve, ve.v)
            for dc in range(8):
                ph.op("tensor", lambda e, t8=t8, dc=dc: e.matmul(
                    PG[:, 0:8], ufm[:, dc * TG + t8 * 128: dc * TG + (t8 + 1) * 128],
                    gwb[:, dc * 8:(dc + 1) * 8], start=(dc == 0), stop=(dc == 7)),
                    inc=pe if dc == 7 else None)
            ph.wait("vector", pe)
            ph.wait("scalar", pe)
            softmax_emit(ph, prb[:], PG[:, 0:8], lncol, pe, ve, ac)
            ph.wait("vector", ve)
            top2_emit(ph, ctx, prb[:], gts[:, t8 * 9:t8 * 9 + 8], lncol[:, 4:8],
                      pe, ve, ac, 0.5)
            ph.op("vector", lambda e, t8=t8: e.memset(
                gts[:, t8 * 9 + 8:t8 * 9 + 9], 1.0 / 16), inc=ve)
        ph.op("vector", lambda e: e.memset(ytg[:], 0.0), inc=ve)
        ph.close()

        for e_i in range(9):
            for h2 in range(2):
                ph = Ph(ctx)
                for fc in range(16):
                    ph.dma("gpsimd", w2b[:, fc * D:(fc + 1) * D],
                           w2e[e_i, h2 * 2048 + fc * 128: h2 * 2048 + (fc + 1) * 128, :], dw)
                w2rd = dw.v
                ev_h = {}
                for fs4 in range(4):
                    fs = h2 * 4 + fs4
                    par = fs % 2
                    nfs = fs + 1
                    ne, ntg2 = e_i, tg
                    if nfs == 8:
                        nfs = 0
                        ne = e_i + 1
                        if ne == 9:
                            ne = 0
                            ntg2 = tg + 1
                    if ntg2 < 4:
                        ph.wait("gpsimd", pe, pe.v)
                        prefetch_w13(ph, ntg2, ne, nfs, 1 - par)
                    ph.wait("tensor", dw, w13rd[(tg, e_i, fs)])
                    for ft4 in range(4):
                        for tns in range(2):
                            it = fs4 * 8 + ft4 * 2 + tns
                            pa = PF[:, (it % 2) * 512:(it % 2) * 512 + 512]
                            pb = PF[:, 1024 + (it % 2) * 512:1024 + (it % 2) * 512 + 512]
                            if it - 2 in ev_h:
                                ph.wait("tensor", ve, ev_h[it - 2])
                            for dc in range(8):
                                ph.op("tensor", lambda e, dc=dc, ft4=ft4, tns=tns, par=par, pa=pa: e.matmul(
                                    pa, w13b[:, par * 8192 + dc * 512 + ft4 * 128:
                                             par * 8192 + dc * 512 + ft4 * 128 + 128],
                                    ufm[:, dc * TG + tns * 512: dc * TG + (tns + 1) * 512],
                                    start=(dc == 0), stop=(dc == 7)))
                            for dc in range(8):
                                ph.op("tensor", lambda e, dc=dc, ft4=ft4, tns=tns, par=par, pb=pb: e.matmul(
                                    pb, w13b[:, par * 8192 + 4096 + dc * 512 + ft4 * 128:
                                             par * 8192 + 4096 + dc * 512 + ft4 * 128 + 128],
                                    ufm[:, dc * TG + tns * 512: dc * TG + (tns + 1) * 512],
                                    start=(dc == 0), stop=(dc == 7)),
                                    inc=pe if dc == 7 else None)
                            ph.wait("scalar", pe)
                            ph.op("scalar", lambda e, pa=pa: e.activation(
                                scr[:], pa, AF.Silu), inc=ac)
                            ph.wait("vector", ac)
                            fidx = fs4 * 4 + ft4
                            ph.op("vector", lambda e, fidx=fidx, tns=tns, pb=pb: e.tensor_tensor(
                                hbuf[:, fidx * TG + tns * 512: fidx * TG + (tns + 1) * 512],
                                scr[:], pb, ALU.mult), inc=ve)
                            ev_h[it] = ve.v
                ph.close()

                ph = Ph(ctx)
                ph.wait("tensor", dw, w2rd)
                ev_w2 = {}
                for t8 in range(8):
                    bank = (t8 % 2) * 1024
                    if t8 >= 2:
                        ph.wait("tensor", ve, ev_w2[t8 - 2])
                    for fc in range(16):
                        for ns in range(2):
                            ph.op("tensor", lambda e, t8=t8, fc=fc, ns=ns, bank=bank: e.matmul(
                                PF[:, bank + ns * 512: bank + (ns + 1) * 512],
                                hbuf[:, fc * TG + t8 * 128: fc * TG + (t8 + 1) * 128],
                                w2b[:, fc * D + ns * 512: fc * D + (ns + 1) * 512],
                                start=(fc == 0), stop=(fc == 15)),
                                inc=pe if (fc == 15 and ns == 1) else None)
                    ph.wait("vector", pe)
                    gcol = gts[:, t8 * 9 + e_i: t8 * 9 + e_i + 1]
                    ph.op("vector", lambda e, t8=t8, bank=bank, gcol=gcol: e.scalar_tensor_tensor(
                        ytg[:, t8 * D:(t8 + 1) * D], PF[:, bank:bank + 1024], gcol,
                        ytg[:, t8 * D:(t8 + 1) * D], ALU.mult, ALU.add), inc=ve)
                    ev_w2[t8] = ve.v
                ph.close()

        ph = Ph(ctx)
        for t8 in range(8):
            tt = tg * 8 + t8
            ph.wait("vector", ac)
            ph.op("vector", lambda e, t8=t8: e.tensor_add(
                sres[:], x3tg[:, t8 * D:(t8 + 1) * D], ytg[:, t8 * D:(t8 + 1) * D]), inc=ve)
            ph.wait("vector", st, st.v)
            ln_emit(ph, sres[:], outb[:, (t8 % 2) * D:(t8 % 2 + 1) * D], lncol, scre[:],
                    pe, ve, ac, epsc[:, 0:1])
            ph.wait("sync", ve)
            ph.dma("sync", out[tt * 128:(tt + 1) * 128, :], outb[:, (t8 % 2) * D:(t8 % 2 + 1) * D], st)
        ph.close()

    ph = Ph(ctx)
    ph.wait("sync", st)
    ph.close()
    return nc, ctx


def prep_l1ffn(inputs, x3_parts):
    w1s = np.asarray(inputs["L1_ffn_w1_s"], np.float32)
    w3s = np.asarray(inputs["L1_ffn_w3_s"], np.float32)
    w2s = np.asarray(inputs["L1_ffn_w2_s"], np.float32)
    w1cat = np.concatenate([w1s[n] for n in range(8)], axis=1)[None]
    w3cat = np.concatenate([w3s[n] for n in range(8)], axis=1)[None]
    w2cat = np.concatenate([w2s[n] for n in range(8)], axis=0)[None]
    shared = dict(
        gwT=np.ascontiguousarray(np.asarray(inputs["L1_ffn_gate_W"], np.float32).T),
        w1e=np.ascontiguousarray(np.concatenate([np.asarray(inputs["L1_ffn_w1_e"], np.float32), w1cat], 0)),
        w3e=np.ascontiguousarray(np.concatenate([np.asarray(inputs["L1_ffn_w3_e"], np.float32), w3cat], 0)),
        w2e=np.ascontiguousarray(np.concatenate([np.asarray(inputs["L1_ffn_w2_e"], np.float32), w2cat], 0)),
        identb=np.eye(128, dtype=BF),
    )
    if x3_parts is None:
        return [dict(shared) for c in range(8)]
    return [dict(shared, x3=x3_parts[c]) for c in range(8)]
